# version 03
# baseline (speedup 1.0000x reference)
import numpy as np
import ml_dtypes
from contextlib import ExitStack
import concourse.bass as bass
import concourse.mybir as mybir
from concourse.bass_utils import run_bass_kernel_spmd

F32 = mybir.dt.float32
BF16 = mybir.dt.bfloat16
FP8 = mybir.dt.float8e4
I32 = mybir.dt.int32
AF = mybir.ActivationFunctionType
ALU = mybir.AluOpType
AX = mybir.AxisListType
bf = ml_dtypes.bfloat16

T = 2048
D = 2048
NT = T // 128
ND = D // 128

LAYERS = (0, 1, 2, 3)
DBG = False



def _merge(*tokens):
    best = {}
    for t in tokens:
        if t is None:
            continue
        if isinstance(t, (list, tuple)) and t and isinstance(t[0], str):
            s, v = t
            if best.get(s, 0) < v:
                best[s] = v
        elif isinstance(t, (list, tuple)):
            for u in t:
                for s, v in _merge(u):
                    if best.get(s, 0) < v:
                        best[s] = v
    return [(s, v) for s, v in best.items()]


_COMPUTE_OPS = {
    "tensor_copy", "tensor_tensor", "tensor_add", "tensor_sub", "tensor_mul",
    "tensor_max", "tensor_scalar", "tensor_single_scalar", "tensor_scalar_add",
    "tensor_scalar_sub", "tensor_scalar_mul", "tensor_scalar_max",
    "tensor_scalar_min", "tensor_relu", "tensor_reduce", "reduce_max",
    "reduce_sum", "scalar_tensor_tensor", "tensor_tensor_scan", "memset",
    "memzero", "reciprocal", "select", "copy_predicated", "transpose",
    "bn_stats", "bn_aggr", "max", "pool", "pool_avg", "pool_max",
    "stream_shuffle", "activation", "sign", "square", "sqrt", "mul", "copy",
    "add", "iota", "affine_select",
}


class EngProxy:

    def __init__(self, prog, sem, eng):
        self._prog, self._sem, self._eng = prog, sem, eng

    def __getattr__(self, m):
        f = getattr(self._eng, m)
        if m in _COMPUTE_OPS:
            prog, sem, eng = self._prog, self._sem, self._eng

            def wrapped(*a, **k):
                tok = prog.flush_auto(sem)
                if tok is not None:
                    eng.wait_ge(prog.semh[tok[0]], tok[1])
                inst = f(*a, **k)
                prog._last[sem] = inst
                return inst
            return wrapped
        return f


class BlockWrap:
    AUTO = {"vector": "aV", "scalar": "aS", "gpsimd": "aQ"}

    def __init__(self, prog, block):
        self._prog, self._block = prog, block

    def __getattr__(self, eng):
        deco = getattr(self._block, eng)
        auto = self.AUTO.get(eng)
        if auto is None:
            return deco
        prog = self._prog

        def mydeco(fn):
            return deco(lambda e, fn=fn: fn(EngProxy(prog, auto, e)))
        return mydeco


class Prog:
    SEMS = ("qS", "qG", "sP", "sV", "sS", "sQ", "aV", "aS", "aQ")

    def __init__(self):
        self.nc = bass.Bass("TRN2", target_bir_lowering=False, debug=False)
        self.stack = ExitStack()
        self.semh = {}
        self.cnt = {}
        self.block = None
        self._n = 0
        self._last = {}
        self._tokmap = {}

    def start(self):
        for s in self.SEMS:
            self.semh[s] = self.stack.enter_context(self.nc.semaphore(s))
            self.cnt[s] = 0
        self.block = BlockWrap(self, self.stack.enter_context(self.nc.Block()))

    def finish(self):
        self.stack.close()

    def inc(self, inst, sem, n=1):
        inst.then_inc(self.semh[sem], n)
        self.cnt[sem] += n
        tok = (sem, self.cnt[sem])
        self._tokmap[id(inst)] = (tok, inst)
        return tok

    def flush_auto(self, autosem):
        prev = self._last.pop(autosem, None)
        if prev is None:
            return None
        hit = self._tokmap.get(id(prev))
        if hit is not None and hit[1] is prev:
            return hit[0]
        return self.inc(prev, autosem)

    def wait(self, eng, *tokens):
        for s, v in _merge(*tokens):
            eng.wait_ge(self.semh[s], v)

    def dma(self, eng, sem, out, in_):
        return self.inc(eng.dma_start(out=out, in_=in_), sem, 16)

    def sb(self, name, shape, dtype):
        return self.stack.enter_context(
            self.nc.sbuf_tensor(f"{name}_{self.uid()}", list(shape), dtype))

    def ps(self, name, shape=(128, 512), dtype=F32):
        return self.stack.enter_context(
            self.nc.psum_tensor(f"{name}_{self.uid()}", list(shape), dtype))

    def dram(self, name, shape, dtype, kind="Internal"):
        return self.nc.dram_tensor(name, list(shape), dtype, kind=kind).ap()

    def uid(self):
        self._n += 1
        return self._n

    def barrier(self):
        for asem in ("aV", "aS", "aQ"):
            self.flush_auto(asem)
        snap = [(k, v) for k, v in self.cnt.items() if v > 0]
        for eng in ("sync", "gpsimd", "tensor", "vector", "scalar"):
            getattr(self.block, eng)(lambda e, snap=snap: self.wait(e, snap))

    from contextlib import contextmanager as _cm

    @_cm
    def scope(self):
        from contextlib import ExitStack as _ES
        st = _ES()
        old = self.stack
        self.stack = st
        try:
            yield
        finally:
            self.barrier()
            self.stack = old
            st.close()


class RR:

    def __init__(self, bufs):
        self.bufs = list(bufs)
        self.toks = [None] * len(self.bufs)
        self.i = 0

    def nxt(self):
        i = self.i
        self.i = (self.i + 1) % len(self.bufs)
        return i, self.bufs[i], self.toks[i]

    def done(self, i, *tok):
        self.toks[i] = _merge(*tok)


class PsumPool:
    def __init__(self, prog, n=8):
        self.rr = RR([prog.ps("pp") for _ in range(n)])

    def acquire(self):
        return self.rr.nxt()

    def release(self, i, *tok):
        self.rr.done(i, *tok)


class FM:

    def __init__(self, prog, name, R, C, dtype):
        self.R, self.C = R, C
        self.nb = R // 128
        self.t = prog.sb(name, (128, self.nb * C), dtype)
        self.wtoks = []

    def blk(self, r, c0=0, c1=None):
        c1 = self.C if c1 is None else c1
        return self.t[:, r * self.C + c0: r * self.C + c1]

    def wrote(self, tok):
        self.wtoks = _merge(self.wtoks, tok)

    def ready(self):
        return self.wtoks


def ceil_div(a, b):
    return (a + b - 1) // b



def emit_transposes(prog, cst, pairs, dtype_b=True):
    block = prog.block
    ident = cst["idb"] if dtype_b else cst["idf"]
    out_toks = []
    for g in range(ceil_div(len(pairs), 4)):
        grp = pairs[4 * g:4 * g + 4]
        bi, bank, ftok = prog.psum_pool.acquire()

        @block.tensor
        def _(t, grp=grp, bank=bank, ftok=ftok):
            prog.wait(t, ftok, *[p[2] for p in grp])
            for j, (src, dst, tk) in enumerate(grp):
                mm = t.matmul(bank[:, 128 * j:128 * (j + 1)], src, ident[:],
                              start=True, stop=True)
            prog._tr = prog.inc(mm, "sP")

        @block.vector
        def _(v, grp=grp, bank=bank, bi=bi):
            prog.wait(v, prog._tr)
            last = None
            for j, (src, dst, tk) in enumerate(grp):
                last = v.tensor_copy(out=dst, in_=bank[:, 128 * j:128 * (j + 1)])
            tk2 = prog.inc(last, "sV")
            prog.psum_pool.release(bi, tk2)
            out_toks.append(tk2)
    return out_toks


def emit_bcast_row(prog, cst, row_ap, width, dst_sb, tok_in, col0=0):
    block = prog.block
    onesr = cst["ones_row_f"]
    toks = []
    for ci in range(ceil_div(width, 512)):
        c0, c1 = 512 * ci, min(width, 512 * ci + 512)
        bi, bank, ftok = prog.psum_pool.acquire()

        @block.tensor
        def _(t, c0=c0, c1=c1, bank=bank, ftok=ftok):
            prog.wait(t, tok_in, ftok)
            mm = t.matmul(bank[:, 0:c1 - c0], onesr[0:1, :], row_ap[0:1, c0:c1],
                          start=True, stop=True)
            prog._bc = prog.inc(mm, "sP")

        @block.vector
        def _(v, c0=c0, c1=c1, bank=bank, bi=bi):
            prog.wait(v, prog._bc)
            tk = prog.inc(v.tensor_copy(out=dst_sb[:, col0 + c0:col0 + c1],
                                        in_=bank[:, 0:c1 - c0]), "sV")
            prog.psum_pool.release(bi, tk)
            toks.append(tk)
    return _merge(toks)


def emit_col_of_row(prog, cst, row_ap, n, dst_col_sb, tok_in):
    block = prog.block
    bi, bank, ftok = prog.psum_pool.acquire()

    @block.tensor
    def _(t):
        prog.wait(t, tok_in, ftok)
        mm = t.matmul(bank[0:n, 0:1], row_ap[0:1, 0:n], cst["ones_row_f"][0:1, 0:1],
                      start=True, stop=True)
        prog._cr = prog.inc(mm, "sP")

    @block.vector
    def _(v):
        prog.wait(v, prog._cr)
        tk = prog.inc(v.tensor_copy(out=dst_col_sb[0:n, 0:1], in_=bank[0:n, 0:1]), "sV")
        prog.psum_pool.release(bi, tk)
        prog._crt = tk
    return prog._crt


def emit_ln(prog, aps, cst, x_hbm, xnT, xn_tm=None, eps_col=None,
            tile_hook=None):
    with prog.scope():
        _emit_ln(prog, aps, cst, x_hbm, xnT, xn_tm, eps_col, tile_hook)


def _emit_ln(prog, aps, cst, x_hbm, xnT, xn_tm=None, eps_col=None,
             tile_hook=None):
    block = prog.block
    bufs = RR([(prog.sb("lnx", (128, D), F32), prog.sb("lnb", (128, D), BF16),
                prog.sb("lnc", (128, 4), F32)) for _ in range(1)])
    xnf = prog.sb("lnxf", (128, D), F32) if tile_hook else None
    eps = eps_col if eps_col is not None else cst["c_eps1e5"]

    for i in range(NT):
        bi, (xa, xb2, cols), ftok = bufs.nxt()

        @block.sync
        def _(s, i=i, xa=xa, ftok=ftok):
            prog.wait(s, ftok)
            prog._ld = prog.dma(s, "qS", xa[:],
                                x_hbm[128 * i:128 * (i + 1), :])

        @block.vector
        def _(v, xa=xa, cols=cols):
            prog.wait(v, prog._ld)
            v.reduce_sum(cols[:, 0:1], xa[:], axis=AX.X)
            prog._m = prog.inc(v.tensor_scalar_mul(cols[:, 0:1], cols[:, 0:1], 1.0 / D), "sV")

        @block.scalar
        def _(s, xa=xa, xb2=xb2, cols=cols, ftok=ftok):
            prog.wait(s, prog._ld, ftok)
            prog._sq = prog.inc(
                s.activation(out=xb2[:], in_=xa[:], func=AF.Square,
                             accum_out=cols[:, 1:2]), "sS")

        @block.vector
        def _(v, cols=cols):
            prog.wait(v, prog._sq, prog._m)
            v.tensor_scalar_mul(cols[:, 1:2], cols[:, 1:2], 1.0 / D)
            v.tensor_tensor(cols[:, 2:3], cols[:, 0:1], cols[:, 0:1], ALU.mult)
            prog._v = prog.inc(
                v.tensor_tensor(cols[:, 1:2], cols[:, 1:2], cols[:, 2:3], ALU.subtract),
                "sV")

        @block.scalar
        def _(s, cols=cols):
            prog.wait(s, prog._v)
            s.activation(out=cols[:, 2:3], in_=cols[:, 1:2], func=AF.Ln,
                         bias=eps[:, 0:1])
            prog._r = prog.inc(
                s.activation(out=cols[:, 2:3], in_=cols[:, 2:3], func=AF.Exp,
                             scale=-0.5), "sS")

        @block.vector
        def _(v, xa=xa, xb2=xb2, cols=cols):
            prog.wait(v, prog._r)
            if xnf is not None:
                v.tensor_scalar(xnf[:], xa[:], cols[:, 0:1], cols[:, 2:3],
                                ALU.subtract, ALU.mult)
                prog._xb = prog.inc(v.tensor_copy(out=xb2[:], in_=xnf[:]), "sV")
            else:
                prog._xb = prog.inc(
                    v.tensor_scalar(xb2[:], xa[:], cols[:, 0:1], cols[:, 2:3],
                                    ALU.subtract, ALU.mult), "sV")

        copy_toks = []
        if xn_tm is not None:
            @block.scalar
            def _(s, i=i, xb2=xb2):
                prog.wait(s, prog._xb)
                tk = prog.inc(s.copy(out=xn_tm.blk(i), in_=xb2[:]), "sS")
                xn_tm.wrote(tk)
                copy_toks.append(tk)

        pairs = [(xb2[:, 128 * c:128 * (c + 1)],
                  xnT.blk(c, 128 * i, 128 * (i + 1)), prog._xb) for c in range(ND)]
        ttoks = emit_transposes(prog, cst, pairs)
        xnT.wrote(ttoks)
        hook_toks = tile_hook(i, xnf, prog._xb) if tile_hook else None
        bufs.done(bi, ttoks, copy_toks, prog._xb, hook_toks)



def emit_fm_block(prog, cst, wT_hbm, wcol0, xnT, dst_sb, K=D, ncols=128,
                  bias_col=None, wdma="qG", name="fmb", dst_free=None,
                  bias_tok=None):
    block = prog.block
    NK = K // 128
    wi, wb_, wtok = prog.wbufs_s.nxt()

    @block.gpsimd
    def _(g):
        prog.wait(g, wtok)
        for k in range(NK):
            prog._wld = prog.dma(g, "qG", wb_[:, ncols * k:ncols * (k + 1)],
                                 wT_hbm[128 * k:128 * (k + 1),
                                        wcol0:wcol0 + ncols])

    toks = []
    for tc in range(T // 512):
        bi, bank, ftok = prog.psum_pool.acquire()

        @block.tensor
        def _(t, tc=tc, bank=bank, ftok=ftok):
            prog.wait(t, prog._wld, ftok, xnT.ready())
            for k in range(NK):
                mm = t.matmul(bank[0:ncols, 0:512],
                              wb_[:, ncols * k:ncols * (k + 1)],
                              xnT.blk(k, 512 * tc, 512 * (tc + 1)),
                              start=(k == 0), stop=(k == NK - 1))
            prog._mm = prog.inc(mm, "sP")

        @block.vector
        def _(v, tc=tc, bank=bank, bi=bi):
            prog.wait(v, prog._mm, dst_free, bias_tok)
            if bias_col is not None:
                tk = prog.inc(v.tensor_single_scalar(
                    dst_sb[0:ncols, 512 * tc:512 * (tc + 1)], bank[0:ncols, 0:512],
                    bias_col, ALU.add), "sV")
            else:
                tk = prog.inc(v.tensor_copy(
                    out=dst_sb[0:ncols, 512 * tc:512 * (tc + 1)],
                    in_=bank[0:ncols, 0:512]), "sV")
            prog.psum_pool.release(bi, tk)
            toks.append(tk)
    prog.wbufs_s.done(wi, prog._mm)
    return _merge(toks)


def emit_tm_proj(prog, cst, wT_hbm, wcol0, ncols, xnT, evac, K=D, name="tmp",
                 fsub=512, in_toks=None):
    block = prog.block
    NK = K // 128
    wbw = 8192
    sub = min(fsub, 512)
    for fs0 in range(0, ncols, sub):
        fs1 = min(ncols, fs0 + sub)
        wi, wb_, wtok = prog.wbufs_l.nxt()

        @block.gpsimd
        def _(g, fs0=fs0, fs1=fs1, wb_=wb_, wtok=wtok):
            prog.wait(g, wtok)
            for k in range(NK):
                prog._wld = prog.dma(g, "qG",
                                     wb_[:, sub * k:sub * k + (fs1 - fs0)],
                                     wT_hbm[128 * k:128 * (k + 1),
                                            wcol0 + fs0:wcol0 + fs1])

        for ti in range(NT):
            bi, bank, ftok = prog.psum_pool.acquire()

            @block.tensor
            def _(t, ti=ti, fs0=fs0, fs1=fs1, wb_=wb_, bank=bank, ftok=ftok):
                prog.wait(t, prog._wld, ftok, xnT.ready(), *(in_toks or []))
                for k in range(NK):
                    mm = t.matmul(bank[:, 0:fs1 - fs0],
                                  xnT.blk(k, 128 * ti, 128 * (ti + 1)),
                                  wb_[:, sub * k:sub * k + (fs1 - fs0)],
                                  start=(k == 0), stop=(k == NK - 1))
                prog._mm = prog.inc(mm, "sP")

            evac(ti, fs0, fs1, bank, bi, prog._mm)
        prog.wbufs_l.done(wi, prog._mm)



def emit_colsum_row(prog, cst, src_sb, width, dst_row, in_tok):
    block = prog.block
    toks = []
    for c0 in range(0, width, 512):
        c1 = min(width, c0 + 512)
        bi, bank, ftok = prog.psum_pool.acquire()

        @block.tensor
        def _(t, c0=c0, c1=c1, bank=bank, ftok=ftok):
            prog.wait(t, in_tok, ftok)
            prog._cs = prog.inc(
                t.matmul(bank[0:1, 0:c1 - c0], cst["ones_col_b"][:, 0:1],
                         src_sb[:, c0:c1], start=True, stop=True), "sP")

        @block.vector
        def _(v, c0=c0, c1=c1, bank=bank, bi=bi):
            prog.wait(v, prog._cs)
            tk = prog.inc(v.tensor_copy(out=dst_row[0:1, c0:c1],
                                        in_=bank[0:1, 0:c1 - c0]), "sV")
            prog.psum_pool.release(bi, tk)
            toks.append(tk)
    return _merge(toks)


def emit_attn_core(prog, cst, heads, kvmap, causal, scale, get_qT, kT, vtok,
                   ctxT):
    block = prog.block
    nkv = kT.R // 128
    sq2 = getattr(prog, "attn_sq2", None)
    if sq2 is None:
        sq2 = prog.sb("sq2", (128, T), BF16)
    prog.attn_sq2 = None
    nrow = prog.sb("nrow", (1, T), F32)
    knrow = nrow
    mk2 = prog.sb("mk2", (1, max(nkv, 2)), F32)
    qnrow = nrow
    cell = prog.sb("cell", (1, 4), F32)
    negM = prog.sb("negM", (128, 1), F32)
    lcol = prog.sb("lcol", (128, 8), F32)
    rl = prog.sb("rl", (128, 1), F32)
    Pb = RR([prog.sb("Pb", (128, T), BF16) for _ in range(1)])
    PTb = RR([prog.sb("PTb", (128, T), BF16) for _ in range(1)])
    ctxs = RR([prog.sb("ctxs", (128, 128), BF16) for _ in range(2)])
    sbank = prog.sbank

    mk2toks = [None] * nkv
    prog._sq2_free = _merge(getattr(prog, "_sq2_free", None),
                            getattr(prog, "_vtr", None))
    for kv in range(nkv):
        @block.scalar
        def _(s, kv=kv):
            prog.wait(s, kT.ready(), getattr(prog, "_sq2_free", None))
            prog._k2 = prog.inc(s.activation(out=sq2[:], in_=kT.blk(kv),
                                             func=AF.Square), "sS")
        cst_tok = emit_colsum_row(prog, cst, sq2, T, knrow, prog._k2)
        prog._sq2_free = cst_tok

        @block.vector
        def _(v, kv=kv, cst_tok=cst_tok):
            prog.wait(v, cst_tok)
            tk = prog.inc(v.reduce_max(mk2[0:1, kv:kv + 1], knrow[0:1, :],
                                       axis=AX.X), "sV")
            mk2toks[kv] = tk

    prog._ex = None
    prog._pv = None
    for h in range(heads):
        qT, qtok = get_qT(h)
        kv = kvmap[h]
        @block.scalar
        def _(s):
            prog.wait(s, qtok, getattr(prog, "_sq2_free", None))
            prog._q2 = prog.inc(s.activation(out=sq2[:], in_=qT[:],
                                             func=AF.Square), "sS")
        qn_tok = emit_colsum_row(prog, cst, sq2, T, qnrow, prog._q2)
        prog._sq2_free = qn_tok

        @block.vector
        def _(v, kv=kv, qn_tok=qn_tok):
            prog.wait(v, qn_tok, mk2toks[kv], prog._ex)
            v.reduce_max(cell[0:1, 0:1], qnrow[0:1, :], axis=AX.X)
            prog._c1 = prog.inc(
                v.tensor_tensor(cell[0:1, 1:2], cell[0:1, 0:1],
                                mk2[0:1, kv:kv + 1], ALU.mult), "sV")

        @block.scalar
        def _(s):
            prog.wait(s, prog._c1)
            prog._c2 = prog.inc(s.activation(out=cell[0:1, 2:3],
                                             in_=cell[0:1, 1:2],
                                             func=AF.Sqrt), "sS")

        bi2, bank2, ftok2 = prog.psum_pool.acquire()

        @block.tensor
        def _(t, bank2=bank2, ftok2=ftok2):
            prog.wait(t, prog._c2, ftok2)
            prog._nm = prog.inc(
                t.matmul(bank2[:, 0:1], cst["ones_row_f"][0:1, :],
                         cell[0:1, 2:3], start=True, stop=True), "sP")

        @block.vector
        def _(v, bi2=bi2, bank2=bank2):
            prog.wait(v, prog._nm, prog._ex)
            tk = prog.inc(v.tensor_scalar_mul(negM[:], bank2[:, 0:1], -scale),
                          "sV")
            prog.psum_pool.release(bi2, tk)
            prog._negM = tk

        for qi in range(NT):
            W = (qi + 1) * 128 if causal else T
            nb = ceil_div(W, 512)
            pi, P_sb, pftok = Pb.nxt()

            @block.tensor
            def _(t, qi=qi, W=W, qT=qT, pftok=pftok):
                prog.wait(t, qtok, kT.ready(), prog._negM, pftok,
                          getattr(prog, "_sbank_free", None))
                for b0 in range(0, W, 512):
                    b1 = min(W, b0 + 512)
                    mm = t.matmul(sbank[:, b0:b1],
                                  qT[:, 128 * qi:128 * (qi + 1)],
                                  kT.blk(kv, b0, b1), start=True, stop=True)
                prog._sc = prog.inc(mm, "sP")

            @block.scalar
            def _(s, W=W, nb=nb, P_sb=P_sb):
                prog.wait(s, prog._sc)
                for b in range(nb):
                    b0, b1 = 512 * b, min(W, 512 * b + 512)
                    acc = None if (causal and b == nb - 1) else lcol[:, b:b + 1]
                    e = s.activation(out=P_sb[:, b0:b1], in_=sbank[:, b0:b1],
                                     func=AF.Exp, scale=scale,
                                     bias=negM[:, 0:1], accum_out=acc)
                prog._ex = prog.inc(e, "sS")

            @block.vector
            def _(v, qi=qi, W=W, nb=nb, P_sb=P_sb):
                prog.wait(v, prog._ex)
                if causal:
                    v.tensor_tensor(P_sb[:, W - 128:W], P_sb[:, W - 128:W],
                                    cst["causal128"][:], ALU.mult)
                    v.reduce_sum(lcol[:, nb - 1:nb], P_sb[:, 512 * (nb - 1):W],
                                 axis=AX.X)
                v.reduce_sum(rl[:], lcol[:, 0:nb], axis=AX.X)
                prog._l = prog.inc(v.reciprocal(rl[:], rl[:]), "sV")
            prog._sbank_free = prog._l

            ti_, PT_sb, ptftok = PTb.nxt()
            pairs = [(P_sb[:, 128 * c:128 * (c + 1)],
                      PT_sb[:, 128 * c:128 * (c + 1)],
                      _merge(prog._l, ptftok)) for c in range(W // 128)]
            ttoks = emit_transposes(prog, cst, pairs)
            Pb.done(pi, ttoks)

            bi3, bank3, ftok3 = prog.psum_pool.acquire()

            @block.tensor
            def _(t, W=W, PT_sb=PT_sb, bank3=bank3, ftok3=ftok3,
                  ttoks=ttoks, kv=kv):
                prog.wait(t, ftok3, ttoks, vtok.ready())
                for c in range(W // 128):
                    mm = t.matmul(bank3[:, 0:128],
                                  PT_sb[:, 128 * c:128 * (c + 1)],
                                  vtok.blk(c, 128 * kv, 128 * (kv + 1)),
                                  start=(c == 0), stop=(c == W // 128 - 1))
                prog._pv = prog.inc(mm, "sP")

            ci, ctx_sb, cftok = ctxs.nxt()

            @block.vector
            def _(v, bank3=bank3, bi3=bi3, ctx_sb=ctx_sb, cftok=cftok):
                prog.wait(v, prog._pv, cftok)
                tk = prog.inc(v.tensor_single_scalar(ctx_sb[:], bank3[:, 0:128],
                                                     rl[:, 0:1], ALU.mult), "sV")
                prog.psum_pool.release(bi3, tk)
                prog._cx = tk

            ct = emit_transposes(prog, cst, [(ctx_sb[:],
                                              ctxT.blk(h, 128 * qi,
                                                       128 * (qi + 1)),
                                              prog._cx)])
            ctxs.done(ci, ct)
            ctxT.wrote(ct)
            PTb.done(ti_, prog._pv)



def emit_preamble(prog, aps):
    block = prog.block
    cst = {}
    cst["idb"] = prog.sb("idb", (128, 128), BF16)
    cst["idf"] = prog.sb("idf", (128, 128), F32)
    cst["ones_col_b"] = prog.sb("ocb", (128, 1), BF16)
    cst["ones_row_f"] = prog.sb("orf", (1, 128), F32)
    cst["ones_col_f"] = prog.sb("ocf", (128, 1), F32)
    cst["ones_row_b"] = prog.sb("orb", (1, 128), BF16)
    cst["causal128"] = prog.sb("cm", (128, 128), BF16)
    cst["triu128"] = prog.sb("tm", (128, 128), BF16)
    cst["ccols"] = prog.sb("ccols", (128, 8), F32)
    cst["c_eps1e5"] = cst["ccols"][:, 0:1]

    @block.sync
    def _(s):
        prog.dma(s, "qS", cst["idb"][:], aps["C_idb"])
        prog.dma(s, "qS", cst["idf"][:], aps["C_idf"])
        prog.dma(s, "qS", cst["ones_col_b"][:], aps["C_ocb"])
        prog.dma(s, "qS", cst["ones_row_f"][:], aps["C_orf"])
        prog.dma(s, "qS", cst["ones_col_f"][:], aps["C_ocf"])
        prog.dma(s, "qS", cst["ones_row_b"][:], aps["C_orb"])
        prog.dma(s, "qS", cst["causal128"][:], aps["C_causal"])
        prog.dma(s, "qS", cst["triu128"][:], aps["C_triu"])
        prog.dma(s, "qS", cst["ccols"][:], aps["C_ccols"])
    prog.barrier()
    return cst


def emit_combine(prog, cst, x_in, s_hbm, g_hbm, x_out):
    block = prog.block
    bufs = RR([(prog.sb("cbx", (128, D), F32), prog.sb("cbs", (128, D), F32),
                prog.sb("cbg", (128, D), F32)) for _ in range(2)])
    for i in range(NT):
        bi, (xa, sa, ga), ftok = bufs.nxt()

        @block.sync
        def _(s, i=i, xa=xa, sa=sa, ga=ga, ftok=ftok):
            prog.wait(s, ftok)
            prog.dma(s, "qS", xa[:], x_in[128 * i:128 * (i + 1), :])
            prog.dma(s, "qS", sa[:], s_hbm[128 * i:128 * (i + 1), :])
            prog._ld3 = prog.dma(s, "qS", ga[:], g_hbm[128 * i:128 * (i + 1), :])

        @block.vector
        def _(v, xa=xa, sa=sa, ga=ga):
            prog.wait(v, prog._ld3)
            v.tensor_tensor(sa[:], sa[:], ga[:], ALU.mult)
            prog._cb = prog.inc(v.tensor_tensor(xa[:], xa[:], sa[:], ALU.add), "sV")

        @block.sync
        def _(s, i=i, xa=xa, bi=bi):
            prog.wait(s, prog._cb)
            bufs.done(bi, prog.dma(s, "qS",
                                   x_out[128 * i:128 * (i + 1), :], xa[:]))



def emit_layer1(prog, aps, cst, x_in, x_out):
    block = prog.block
    H = 16
    u = prog.uid()
    kind = "ExternalOutput" if DBG else "Internal"
    s_hbm = prog.dram("dbg_s" if DBG else f"l1s_{u}", (T, D), F32, kind=kind)
    g_hbm = prog.dram("dbg_g" if DBG else f"l1g_{u}", (T, D), F32, kind=kind)
    h_hbm = prog.dram("dbg_h" if DBG else f"l1h_{u}", (NT, 128, 8192), BF16,
                      kind=kind)
    hsT_hbm = prog.dram(f"l1hsT_{u}", (NT, 128, 8192), BF16)

    with prog.scope():
        xnT = FM(prog, "xnT", D, T, BF16)
        emit_ln(prog, aps, cst, x_in, xnT)

        with prog.scope():
            ctxT = FM(prog, "ctxT", D, T, BF16)
            with prog.scope():
                emit_l1_attn(prog, aps, cst, xnT, ctxT)
            with prog.scope():
                emit_l1_outproj(prog, aps, cst, ctxT, s_hbm)

        emit_l1_gateup(prog, aps, cst, xnT, h_hbm, x_in)

    with prog.scope():
        emit_l1_sparsify(prog, aps, cst, h_hbm, hsT_hbm)
    with prog.scope():
        emit_l1_down(prog, aps, cst, hsT_hbm, g_hbm)

    emit_combine(prog, cst, x_in, s_hbm, g_hbm, x_out)


def emit_l1_attn(prog, aps, cst, xnT, ctxT):
    block = prog.block
    H = 16
    if True:
        if True:
            prog.wbufs_s = RR([prog.sb("wbs", (128, 2048), BF16)
                               for _ in range(2)])
            kT = FM(prog, "kT", 128, T, BF16)
            vtok = FM(prog, "vtok", T, 128, BF16)
            bias_sb = prog.sb("qkvb", (128, 18), F32)

            @block.sync
            def _(s):
                prog._bld = prog.dma(s, "qS", bias_sb[:], aps["L1_qkvb"])

            ktk = emit_fm_block(prog, cst, aps["L1_WqkvT"], 2048, xnT, kT.t,
                                bias_col=bias_sb[:, 16:17], bias_tok=prog._bld)
            kT.wrote(ktk)
            vtmp = prog.sb("vtmp", (128, T), BF16)
            vtk = emit_fm_block(prog, cst, aps["L1_WqkvT"], 2176, xnT, vtmp,
                                bias_col=bias_sb[:, 17:18], bias_tok=prog._bld)
            pairs = [(vtmp[:, 128 * i:128 * (i + 1)], vtok.blk(i), vtk)
                     for i in range(NT)]
            vtok.wrote(emit_transposes(prog, cst, pairs))

            qbufs = RR([prog.sb("qh", (128, T), BF16) for _ in range(2)])
            qstate = {}

            def get_qT(h):
                if qstate.get("pending") is not None:
                    qbufs.done(qstate["pending"],
                               getattr(prog, "_sc", None),
                               getattr(prog, "_q2", None))
                qi_, qb, qftok = qbufs.nxt()
                qstate["pending"] = qi_
                tok = emit_fm_block(prog, cst, aps["L1_WqkvT"], 128 * h, xnT, qb,
                                    bias_col=bias_sb[:, h:h + 1], dst_free=qftok,
                                    bias_tok=prog._bld)
                return qb, tok

            emit_attn_core(prog, cst, H, [0] * H, True, 1.0 / np.sqrt(128.0),
                           get_qT, kT, vtok, ctxT)


def emit_l1_outproj(prog, aps, cst, ctxT, s_hbm):
    block = prog.block
    if True:
        if True:
            prog.wbufs_l = RR([prog.sb("wbl", (128, 8192), BF16)
                               for _ in range(2)])
            obrow = prog.sb("obrow", (1, D), F32)
            obb = prog.sb("obb", (128, D), F32)

            @block.sync
            def _(s):
                prog._obl = prog.dma(s, "qS", obrow[:], aps["L1_ob"])
            obtok = emit_bcast_row(prog, cst, obrow, D, obb, prog._obl)
            spieces = RR([prog.sb("spiece", (128, 512), F32) for _ in range(3)])

            def evac_s(ti, f0, f1, bank, bi, mtok):
                pi, piece, pftok = spieces.nxt()

                @block.vector
                def _(v):
                    prog.wait(v, mtok, pftok, obtok)
                    prog._sv = prog.inc(
                        v.scalar_tensor_tensor(piece[:, 0:f1 - f0], bank[:, 0:f1 - f0],
                                               0.0, obb[:, f0:f1], ALU.add, ALU.add),
                        "sV")
                    prog.psum_pool.release(bi, prog._sv)

                @block.sync
                def _(s):
                    prog.wait(s, prog._sv)
                    spieces.done(pi, prog.dma(s, "qS",
                                              s_hbm[128 * ti:128 * (ti + 1), f0:f1],
                                              piece[:, 0:f1 - f0]))

            emit_tm_proj(prog, cst, aps["L1_outT"], 0, D, ctxT, evac_s)


def emit_l1_gateup(prog, aps, cst, xnT, h_hbm, x_in):
    block = prog.block
    if True:
        with prog.scope():
            xnT_lo = FM(prog, "xnTlo", D, T, BF16)
            with prog.scope():
                lob = prog.sb("lob", (128, D), BF16)
                bufs = RR([(prog.sb("lx", (128, D), F32),
                            prog.sb("lf", (128, D), F32),
                            prog.sb("lc", (128, 4), F32)) for _ in range(1)])
                for i in range(NT):
                    bi_, (xa, xf, cols), ftok = bufs.nxt()

                    @block.sync
                    def _(sy, i=i, xa=xa, ftok=ftok):
                        prog.wait(sy, ftok)
                        prog._lxl = prog.dma(sy, "qS", xa[:],
                                             x_in[128 * i:128 * (i + 1), :])

                    @block.vector
                    def _(v, xa=xa, cols=cols):
                        prog.wait(v, prog._lxl)
                        v.reduce_sum(cols[:, 0:1], xa[:], axis=AX.X)
                        prog._lm = prog.inc(v.tensor_scalar_mul(
                            cols[:, 0:1], cols[:, 0:1], 1.0 / D), "sV")

                    @block.scalar
                    def _(sc, xa=xa, xf=xf, cols=cols):
                        prog.wait(sc, prog._lm)
                        prog._lsq = prog.inc(
                            sc.activation(out=xf[:], in_=xa[:], func=AF.Square,
                                          accum_out=cols[:, 1:2]), "sS")

                    @block.vector
                    def _(v, cols=cols):
                        prog.wait(v, prog._lsq)
                        v.tensor_scalar_mul(cols[:, 1:2], cols[:, 1:2], 1.0 / D)
                        v.tensor_tensor(cols[:, 2:3], cols[:, 0:1], cols[:, 0:1],
                                        ALU.mult)
                        prog._lv = prog.inc(v.tensor_tensor(
                            cols[:, 1:2], cols[:, 1:2], cols[:, 2:3],
                            ALU.subtract), "sV")

                    @block.scalar
                    def _(sc, cols=cols):
                        prog.wait(sc, prog._lv)
                        sc.activation(out=cols[:, 2:3], in_=cols[:, 1:2],
                                      func=AF.Ln, bias=cst["c_eps1e5"][:, 0:1])
                        prog._lr = prog.inc(sc.activation(
                            out=cols[:, 2:3], in_=cols[:, 2:3], func=AF.Exp,
                            scale=-0.5), "sS")

                    @block.vector
                    def _(v, i=i, xa=xa, xf=xf, cols=cols):
                        prog.wait(v, prog._lr, getattr(prog, "_lotr", None))
                        v.tensor_scalar(xf[:], xa[:], cols[:, 0:1], cols[:, 2:3],
                                        ALU.subtract, ALU.mult)
                        v.tensor_copy(out=lob[:], in_=xf[:])
                        prog._lo = prog.inc(
                            v.scalar_tensor_tensor(lob[:], xf[:], 0.0, lob[:],
                                                   ALU.add, ALU.subtract), "sV")

                    pairs = [(lob[:, 128 * c:128 * (c + 1)],
                              xnT_lo.blk(c, 128 * i, 128 * (i + 1)), prog._lo)
                             for c in range(ND)]
                    tt = emit_transposes(prog, cst, pairs)
                    xnT_lo.wrote(tt)
                    prog._lotr = tt
                    bufs.done(bi_, tt, prog._lo)

            wg_b = RR([prog.sb("wgb", (128, 8192), BF16) for _ in range(1)])
            wu_b = RR([prog.sb("wub", (128, 8192), BF16) for _ in range(1)])
            wglo = prog.sb("wglo", (128, 8192), BF16)
            wulo = prog.sb("wulo", (128, 8192), BF16)
            wraw = RR([prog.sb("wr1", (128, 512), F32) for _ in range(2)])
            gs = RR([prog.sb("gs", (128, 512), F32) for _ in range(1)])
            us = RR([prog.sb("us", (128, 512), F32) for _ in range(1)])
            hp = RR([prog.sb("hp", (128, 512), BF16) for _ in range(2)])
            for fb in range(16):
                wgi, wg_, wgtok = wg_b.nxt()
                wui, wu_, wutok = wu_b.nxt()
                qtoks = []
                for (nm, hib, lob_) in (("L1_wg", wg_, wglo),
                                        ("L1_wu", wu_, wulo)):
                    for k in range(16):
                        wri, wr_, wrtok = wraw.nxt()

                        @block.sync
                        def _(sy, nm=nm, fb=fb, k=k, wr_=wr_, wrtok=wrtok):
                            prog.wait(sy, wrtok)
                            prog._wrl = prog.dma(
                                sy, "qS", wr_[:],
                                aps[nm][128 * k:128 * (k + 1),
                                        512 * fb:512 * (fb + 1)])

                        @block.vector
                        def _(v, k=k, wr_=wr_, wri=wri, hib=hib, lob_=lob_,
                              wgtok=wgtok, wutok=wutok):
                            prog.wait(v, prog._wrl, wgtok, wutok,
                                      getattr(prog, "_wql", None))
                            v.tensor_copy(out=hib[:, 512 * k:512 * (k + 1)],
                                          in_=wr_[:])
                            tk = prog.inc(v.scalar_tensor_tensor(
                                lob_[:, 512 * k:512 * (k + 1)], wr_[:], 0.0,
                                hib[:, 512 * k:512 * (k + 1)],
                                ALU.add, ALU.subtract), "sV")
                            qtoks.append(tk)
                            wraw.done(wri, tk)
                            prog._wql = tk

                for ti in range(NT):
                    bgi, bankg, ftokg = prog.psum_pool.acquire()
                    bui, banku, ftoku = prog.psum_pool.acquire()

                    @block.tensor
                    def _(t, ti=ti, wg_=wg_, wu_=wu_, bankg=bankg, banku=banku,
                          ftokg=ftokg, ftoku=ftoku, qtoks=qtoks):
                        prog.wait(t, qtoks, ftokg, ftoku, xnT.ready(),
                                  xnT_lo.ready())
                        for bank, whi, wlo in ((bankg, wg_, wglo),
                                               (banku, wu_, wulo)):
                            first = True
                            for xsrc, wsrc in ((xnT, whi), (xnT, wlo),
                                               (xnT_lo, whi)):
                                for k in range(16):
                                    mm = t.matmul(
                                        bank[:, 0:512],
                                        xsrc.blk(k, 128 * ti, 128 * (ti + 1)),
                                        wsrc[:, 512 * k:512 * (k + 1)],
                                        start=first,
                                        stop=(xsrc is xnT_lo and k == 15))
                                    first = False
                        prog._mgu = prog.inc(mm, "sP")

                    gi_, gbuf, gftok = gs.nxt()
                    ui_, ubuf, uftok = us.nxt()
                    hi_, hbuf, hftok = hp.nxt()

                    @block.scalar
                    def _(s, bankg=bankg, gbuf=gbuf, gftok=gftok):
                        prog.wait(s, prog._mgu, gftok)
                        prog._sg = prog.inc(
                            s.activation(out=gbuf[:], in_=bankg[:, 0:512],
                                         func=AF.Silu), "sS")

                    @block.vector
                    def _(v, banku=banku, ubuf=ubuf, uftok=uftok, bgi=bgi,
                          bui=bui, gbuf=gbuf, hbuf=hbuf, hftok=hftok):
                        prog.wait(v, prog._mgu, uftok, hftok)
                        tk = prog.inc(v.tensor_copy(out=ubuf[:], in_=banku[:, 0:512]),
                                      "sV")
                        prog.psum_pool.release(bui, tk)
                        prog.wait(v, prog._sg)
                        hv = hbuf[:].rearrange("p (l g) -> p g l", l=4)
                        tk2 = prog.inc(v.tensor_tensor(hv, gbuf[:].rearrange(
                            "p (g l) -> p g l", l=4), ubuf[:].rearrange(
                            "p (g l) -> p g l", l=4), ALU.mult), "sV")
                        prog.psum_pool.release(bgi, tk2)
                        prog._hp = tk2
                        gs.done(gi_, tk2)
                        us.done(ui_, tk2)

                    @block.sync
                    def _(s, fb=fb, ti=ti, hbuf=hbuf, hi_=hi_):
                        prog.wait(s, prog._hp)
                        dst = h_hbm[ti].rearrange("p (l g) -> p l g", l=4)[
                            :, :, 128 * fb:128 * (fb + 1)]
                        hp.done(hi_, prog.dma(
                            s, "qS", dst,
                            hbuf[:].rearrange("p (l g) -> p l g", l=4)))
                wg_b.done(wgi, prog._mgu)
                wu_b.done(wui, prog._mgu)
                prog._wql = _merge(prog._wql, prog._mgu)

def emit_l1_sparsify(prog, aps, cst, h_hbm, hsT_hbm):
    block = prog.block
    if True:
        if True:
            hl = RR([prog.sb("hl", (128, 8192), BF16) for _ in range(2)])
            habs = prog.sb("habs", (128, 8192), BF16)
            mnx = prog.sb("mnx", (128, 8192), BF16)
            t2 = prog.sb("t2", (128, 2048), BF16)
            msk = prog.sb("msk", (128, 8192), BF16)
            hsl = RR([prog.sb("hsl", (128, 8192), BF16) for _ in range(2)])
            hsTb = RR([prog.sb("hsTt", (128, 8192), BF16) for _ in range(2)])
            for ti in range(NT):
                li, hbuf, lftok = hl.nxt()

                @block.sync
                def _(s, ti=ti, hbuf=hbuf, lftok=lftok):
                    prog.wait(s, lftok)
                    prog._hld = prog.dma(s, "qS", hbuf[:], h_hbm[ti])

                @block.scalar
                def _(s, hbuf=hbuf):
                    prog.wait(s, prog._hld, getattr(prog, "_spf", None))
                    prog._ha = prog.inc(
                        s.activation(out=habs[:], in_=hbuf[:], func=AF.Abs), "sS")

                si, hsb, sftok = hsl.nxt()

                @block.vector
                def _(v, hbuf=hbuf, hsb=hsb, sftok=sftok):
                    prog.wait(v, prog._ha, sftok)
                    L_ = lambda l: habs[:, 2048 * l:2048 * (l + 1)]
                    M_ = lambda q_: mnx[:, 2048 * q_:2048 * (q_ + 1)]
                    v.tensor_tensor(M_(0), L_(0), L_(1), ALU.max)
                    v.tensor_tensor(M_(1), L_(0), L_(1), ALU.min)
                    v.tensor_tensor(M_(2), L_(2), L_(3), ALU.max)
                    v.tensor_tensor(M_(3), L_(2), L_(3), ALU.min)
                    v.tensor_tensor(M_(0), M_(0), M_(2), ALU.min)
                    v.tensor_tensor(M_(1), M_(1), M_(3), ALU.max)
                    v.tensor_tensor(t2[:], M_(0), M_(1), ALU.max)
                    v.tensor_tensor(
                        msk[:].rearrange("p (l g) -> p l g", l=4),
                        habs[:].rearrange("p (l g) -> p l g", l=4),
                        t2[:].unsqueeze(1).to_broadcast([128, 4, 2048]),
                        ALU.is_ge)
                    prog._spf = prog.inc(
                        v.tensor_tensor(hsb[:], hbuf[:], msk[:], ALU.mult), "sV")
                hl.done(li, prog._spf)

                tti, hsT_t, tftok = hsTb.nxt()
                pairs = [(hsb[:, 128 * c:128 * (c + 1)],
                          hsT_t[:, 128 * c:128 * (c + 1)],
                          _merge(prog._spf, tftok)) for c in range(64)]
                ttoks = emit_transposes(prog, cst, pairs)
                hsl.done(si, ttoks)

                @block.sync
                def _(s, ti=ti, hsT_t=hsT_t, ttoks=ttoks):
                    prog.wait(s, ttoks)
                    prog._hst = prog.dma(s, "qS", hsT_hbm[ti], hsT_t[:])
                hsTb.done(tti, prog._hst)

def emit_l1_down(prog, aps, cst, hsT_hbm, g_hbm):
    block = prog.block
    if True:
        if True:
            wd = prog.sb("wd", (128, 64 * 512), BF16)
            wdtok = [None]
            hst = RR([prog.sb("hst", (128, 8192), BF16) for _ in range(2)])
            gp = RR([prog.sb("gp", (128, 512), F32) for _ in range(3)])
            for fbd in range(4):
                @block.gpsimd
                def _(g, fbd=fbd):
                    prog.wait(g, wdtok[0])
                    for ic in range(64):
                        prog._wdl = prog.dma(
                            g, "qG", wd[:, 512 * ic:512 * (ic + 1)],
                            aps["L1_wdp"][128 * ic:128 * (ic + 1),
                                          512 * fbd:512 * (fbd + 1)])

                for ti in range(NT):
                    hi_, hst_, hftok = hst.nxt()

                    @block.sync
                    def _(s, ti=ti, hst_=hst_, hftok=hftok):
                        prog.wait(s, hftok, prog._hst)
                        prog._hl2 = prog.dma(s, "qS", hst_[:], hsT_hbm[ti])

                    bi, bank, ftok = prog.psum_pool.acquire()

                    @block.tensor
                    def _(t, hst_=hst_, bank=bank, ftok=ftok):
                        prog.wait(t, prog._wdl, prog._hl2, ftok)
                        for ic in range(64):
                            mm = t.matmul(bank[:, 0:512],
                                          hst_[:, 128 * ic:128 * (ic + 1)],
                                          wd[:, 512 * ic:512 * (ic + 1)],
                                          start=(ic == 0), stop=(ic == 63))
                        prog._md = prog.inc(mm, "sP")
                    hst.done(hi_, prog._md)

                    gi_, gbuf, gftok = gp.nxt()

                    @block.scalar
                    def _(s, bank=bank, bi=bi, gbuf=gbuf, gftok=gftok):
                        prog.wait(s, prog._md, gftok)
                        tk = prog.inc(s.activation(out=gbuf[:], in_=bank[:, 0:512],
                                                   func=AF.Silu), "sS")
                        prog.psum_pool.release(bi, tk)
                        prog._gs2 = tk

                    @block.sync
                    def _(s, ti=ti, fbd=fbd, gbuf=gbuf, gi_=gi_):
                        prog.wait(s, prog._gs2)
                        gp.done(gi_, prog.dma(
                            s, "qS", g_hbm[128 * ti:128 * (ti + 1),
                                           512 * fbd:512 * (fbd + 1)], gbuf[:]))
                wdtok[0] = prog._md



def declare_inputs(prog, arrays):
    aps = {}
    for name, arr in arrays.items():
        dt = {np.dtype(np.float32): F32, np.dtype(np.int32): I32,
              np.dtype(bf): BF16}[arr.dtype]
        aps[name] = prog.dram(name, arr.shape, dt, kind="ExternalInput")
    return aps


def _f32(a):
    return np.ascontiguousarray(np.asarray(a), dtype=np.float32)


def stage_consts():
    c = {}
    c["C_idb"] = np.eye(128, dtype=bf)
    c["C_idf"] = np.eye(128, dtype=np.float32)
    c["C_ocb"] = np.ones((128, 1), dtype=bf)
    c["C_orf"] = np.ones((1, 128), dtype=np.float32)
    c["C_ocf"] = np.ones((128, 1), dtype=np.float32)
    c["C_orb"] = np.ones((1, 128), dtype=bf)
    c["C_causal"] = np.tril(np.ones((128, 128), dtype=np.float32)).astype(bf)
    c["C_triu"] = np.triu(np.ones((128, 128), dtype=np.float32)).astype(bf)
    ccols = np.zeros((128, 8), dtype=np.float32)
    ccols[:, 0] = 1e-5
    c["C_ccols"] = ccols
    return c


def stage_layer1(inputs):
    a = {}
    a["L1_WqkvT"] = _f32(inputs["L1_attn_Wqkv_w"]).T.copy()
    a["L1_qkvb"] = _f32(inputs["L1_attn_Wqkv_b"]).reshape(18, 128).T.copy()
    a["L1_outT"] = _f32(inputs["L1_attn_out_w"]).T.copy()
    a["L1_ob"] = _f32(inputs["L1_attn_out_b"]).reshape(1, 2048).copy()
    a["L1_wg"] = _f32(inputs["L1_ffn_w_gate"])
    a["L1_wu"] = _f32(inputs["L1_ffn_w_up"])
    wd = _f32(inputs["L1_ffn_w_down"])
    a["L1_wdp"] = wd.reshape(2048, 4, 2048).transpose(1, 0, 2).reshape(8192, 2048).copy()
    return a


LAYER_STAGERS = {1: stage_layer1}
LAYER_EMITTERS = {}


def _build_one(layers, weight_arrays):
    prog = Prog()
    prog.start()
    aps = declare_inputs(prog, weight_arrays)
    x_in = prog.dram("x_in", (T, D), F32, kind="ExternalInput")
    x_out = prog.dram("x_out", (T, D), F32, kind="ExternalOutput")
    prog.psum_pool = PsumPool(prog, 4)
    prog.sbank = prog.ps("sbank", (128, 2048))
    cst = emit_preamble(prog, aps)
    cur_in = x_in
    scratch = []
    for j, L in enumerate(layers):
        out = x_out if j == len(layers) - 1 else prog.dram(
            f"xmid{j}", (T, D), F32)
        LAYER_EMITTERS[L](prog, aps, cst, cur_in, out)
        cur_in = out
    prog.barrier()
    prog.finish()
    return prog


LAYER_EMITTERS[1] = emit_layer1


def kernel(**inputs):
    B = 8
    x = _f32(inputs["L0_attn_hidden_states"])
    consts = stage_consts()
    x_cur = [np.ascontiguousarray(x[b]) for b in range(B)]
    groups = [[L] for L in LAYERS]
    for layers in groups:
        arrays = dict(consts)
        for L in layers:
            arrays.update(LAYER_STAGERS[L](inputs))
        prog = _build_one(layers, arrays)
        in_maps = [dict(arrays, x_in=x_cur[b]) for b in range(B)]
        res = run_bass_kernel_spmd(prog.nc, in_maps, core_ids=list(range(B)))
        if DBG:
            global DBG_OUT
            DBG_OUT = res.results[0]
        x_cur = [np.asarray(r["x_out"], dtype=np.float32) for r in res.results]
    return np.stack(x_cur, axis=0)



def emit_rope_fm(prog, cst, src, dst, cosT, sinTs, tok, style, free0=0,
                 free1=T):
    block = prog.block

    @block.vector
    def _(v):
        prog.wait(v, tok)
        tmp = prog._rope_tmp
        n = free1 - free0
        v.tensor_copy(out=tmp[0:64, 0:n], in_=src[64:128, free0:free1])
        v.tensor_copy(out=tmp[64:128, 0:n], in_=src[0:64, free0:free1])
        v.tensor_tensor(tmp[:, 0:n], tmp[:, 0:n], sinTs[:, free0:free1],
                        ALU.mult)
        v.tensor_tensor(dst[:, free0:free1], src[:, free0:free1],
                        cosT[:, free0:free1], ALU.mult)
        prog._rp = prog.inc(
            v.tensor_tensor(dst[:, free0:free1], dst[:, free0:free1],
                            tmp[:, 0:n], ALU.add), "sV")
    return prog._rp


def emit_layer2(prog, aps, cst, x_in, x_out):
    block = prog.block
    u = prog.uid()
    kind = "ExternalOutput" if DBG else "Internal"
    s_hbm = prog.dram("dbg_s" if DBG else f"l2s_{u}", (T, D), F32, kind=kind)
    g_hbm = prog.dram("dbg_g" if DBG else f"l2g_{u}", (T, D), F32, kind=kind)
    h1T_hbm = prog.dram(f"l2h1T_{u}", (NT, 128, 8192), BF16)

    with prog.scope():
        xnT = FM(prog, "xnT", D, T, BF16)
        emit_ln(prog, aps, cst, x_in, xnT)

        with prog.scope():
            ctxT = FM(prog, "ctxT", D, T, BF16)
            with prog.scope():
                emit_l2_attn(prog, aps, cst, xnT, ctxT)
            with prog.scope():
                prog.wbufs_l = RR([prog.sb("wbl", (128, 8192), BF16)
                                   for _ in range(2)])
                spieces = RR([prog.sb("spiece", (128, 512), F32)
                              for _ in range(3)])

                def evac_s(ti, f0, f1, bank, bi, mtok):
                    pi, piece, pftok = spieces.nxt()

                    @block.vector
                    def _(v):
                        prog.wait(v, mtok, pftok)
                        prog._sv = prog.inc(
                            v.tensor_copy(out=piece[:, 0:f1 - f0],
                                          in_=bank[:, 0:f1 - f0]), "sV")
                        prog.psum_pool.release(bi, prog._sv)

                    @block.sync
                    def _(s):
                        prog.wait(s, prog._sv)
                        spieces.done(pi, prog.dma(
                            s, "qS", s_hbm[128 * ti:128 * (ti + 1), f0:f1],
                            piece[:, 0:f1 - f0]))

                emit_tm_proj(prog, cst, aps["L2_wo"], 0, D, ctxT, evac_s)

        with prog.scope():
            emit_l2_ffn1(prog, aps, cst, xnT, h1T_hbm)

    with prog.scope():
        emit_l2_ffn2(prog, aps, cst, h1T_hbm, g_hbm)

    emit_combine(prog, cst, x_in, s_hbm, g_hbm, x_out)


def emit_l2_attn(prog, aps, cst, xnT, ctxT):
    block = prog.block
    H, NKV = 16, 4
    prog.wbufs_s = RR([prog.sb("wbs", (128, 2048), BF16) for _ in range(1)])
    kT = FM(prog, "kT", 512, T, BF16)
    vtok = FM(prog, "vtok", T, 512, BF16)
    cosT = prog.sb("cosT", (128, T), BF16)
    sinT = prog.sb("sinT", (128, T), BF16)
    prog._rope_tmp = prog.sb("rtmp", (128, T), BF16)
    fmtmp = prog.sb("fmtmp", (128, T), BF16)
    prog.attn_sq2 = fmtmp

    @block.sync
    def _(s):
        prog.dma(s, "qS", cosT[:], aps["C_l2cosT"])
        prog._rld = prog.dma(s, "qS", sinT[:], aps["C_l2sinT"])

    kraw = fmtmp
    for kv in range(NKV):
        ktk = emit_fm_block(prog, cst, aps["L2_wk"], 128 * kv, xnT, kraw,
                            dst_free=getattr(prog, "_rp", None))
        rtk = emit_rope_fm(prog, cst, kraw, kT.t[:, kv * T:(kv + 1) * T],
                           cosT, sinT, _merge(ktk, prog._rld), "l2")
        kT.wrote(rtk)

    vtmp = fmtmp
    for kv in range(4):
        vtk = emit_fm_block(prog, cst, aps["L2_wv"], 128 * kv, xnT, vtmp,
                            dst_free=_merge(getattr(prog, "_vtr", None),
                                            getattr(prog, "_rp", None)))
        pairs = [(vtmp[:, 128 * i:128 * (i + 1)],
                  vtok.blk(i, 128 * kv, 128 * (kv + 1)), vtk)
                 for i in range(NT)]
        tt = emit_transposes(prog, cst, pairs)
        prog._vtr = tt
        vtok.wrote(tt)

    qbufs = RR([prog.sb("qh", (128, T), BF16) for _ in range(1)])
    qrot = RR([prog.sb("qr", (128, T), BF16) for _ in range(1)])
    qstate = {}

    def get_qT(h):
        if qstate.get("pending") is not None:
            pq, pr = qstate["pending"]
            qbufs.done(pq, prog._rp)
            qrot.done(pr, _merge(getattr(prog, "_sc", None),
                                 getattr(prog, "_q2", None)))
        qi_, qb, qftok = qbufs.nxt()
        ri_, qr, rftok = qrot.nxt()
        qstate["pending"] = (qi_, ri_)
        tok = emit_fm_block(prog, cst, aps["L2_wq"], 128 * h, xnT, qb,
                            dst_free=qftok)
        rtk = emit_rope_fm(prog, cst, qb, qr, cosT, sinT,
                           _merge(tok, prog._rld, rftok), "l2")
        return qr, rtk

    emit_attn_core(prog, cst, H, [h // 4 for h in range(H)], False,
                   1.0 / np.sqrt(128.0), get_qT, kT, vtok, ctxT)


def emit_ternary_scale(prog, aps, cst, wT_hbm, K_, F_, kind="mean_abs"):
    block = prog.block
    NKr = K_ // 128
    cellr = prog.sb("tscell", (1, 4), F32)
    prog._ts_ctx = prog.scope()
    prog._ts_ctx.__enter__()
    acc = prog.sb("tsacc", (128, 1), F32)
    tmp = prog.sb("tstmp", (128, 1), F32)
    if kind == "rms":
        prog._tssq = prog.sb("tssq", (128, F_), F32)
    wstrip = RR([prog.sb("tsw", (128, F_), F32) for _ in range(2)])
    first = [True]
    for k in range(NKr):
        wi, ws, wtok = wstrip.nxt()

        @block.sync
        def _(s, k=k, ws=ws, wtok=wtok):
            prog.wait(s, wtok)
            prog._tl = prog.dma(s, "qS", ws[:], wT_hbm[128 * k:128 * (k + 1), :])

        @block.vector
        def _(v, ws=ws, wi=wi):
            prog.wait(v, prog._tl)
            if kind == "mean_abs":
                v.reduce_sum(tmp[:], ws[:], axis=AX.X, apply_absolute_value=True)
            else:
                v.tensor_tensor(prog._tssq[:], ws[:], ws[:], ALU.mult)
                v.reduce_sum(tmp[:], prog._tssq[:], axis=AX.X)
            if first[0]:
                tk = prog.inc(v.tensor_copy(out=acc[:], in_=tmp[:]), "sV")
                first[0] = False
            else:
                tk = prog.inc(v.tensor_tensor(acc[:], acc[:], tmp[:], ALU.add),
                              "sV")
            wstrip.done(wi, tk)
            prog._ts1 = tk

    bi, bank, ftok = prog.psum_pool.acquire()

    @block.tensor
    def _(t):
        prog.wait(t, prog._ts1, ftok)
        prog._ts2 = prog.inc(
            t.matmul(bank[0:1, 0:1], cst["ones_col_f"][:, 0:1], acc[:, 0:1],
                     start=True, stop=True), "sP")

    @block.vector
    def _(v):
        prog.wait(v, prog._ts2)
        tk = prog.inc(v.tensor_scalar_mul(cellr[0:1, 0:1], bank[0:1, 0:1],
                                          1.0 / (K_ * F_)), "sV")
        prog.psum_pool.release(bi, tk)
        prog._ts3 = tk
    prog._ts_ctx.__exit__(None, None, None)
    return cellr, prog._ts3


def emit_quant_col(prog, cst, scell, stok, mulval):
    block = prog.block
    rcell = prog.sb("qc_r", (1, 2), F32)

    @block.vector
    def _(v):
        prog.wait(v, stok)
        prog._qc1 = prog.inc(v.reciprocal(rcell[0:1, 0:1], scell[0:1, 0:1]), "sV")

    col2 = prog.sb("qc_c", (128, 2), F32)
    bi, bank, ftok = prog.psum_pool.acquire()

    @block.tensor
    def _(t):
        prog.wait(t, prog._qc1, ftok)
        t.matmul(bank[:, 0:1], cst["ones_row_f"][0:1, :], rcell[0:1, 0:1],
                 start=True, stop=True)
        prog._qc2 = prog.inc(
            t.matmul(bank[:, 1:2], cst["ones_row_f"][0:1, :], scell[0:1, 0:1],
                     start=True, stop=True), "sP")

    @block.vector
    def _(v):
        prog.wait(v, prog._qc2)
        v.tensor_scalar_mul(col2[:, 0:1], bank[:, 0:1], float(mulval))
        tk = prog.inc(v.tensor_copy(out=col2[:, 1:2], in_=bank[:, 1:2]), "sV")
        prog.psum_pool.release(bi, tk)
        prog._qc3 = tk
    return col2, prog._qc3


def emit_ternary_quant_piece(prog, cst, wraw_ap, wq_ap, rs_col, qtok, scratch):
    block = prog.block
    a, sgn = scratch

    @block.scalar
    def _(s):
        prog.wait(s, qtok, getattr(prog, "_tq_free", None))
        s.activation(out=a[:, 0:wraw_ap.shape[1]], in_=wraw_ap, func=AF.Abs,
                     scale=rs_col[:, 0:1])
        prog._tq1 = prog.inc(s.sign(out=sgn[:, 0:wraw_ap.shape[1]],
                                    in_=wraw_ap), "sS")

    @block.vector
    def _(v):
        prog.wait(v, prog._tq1)
        prog._tq2 = prog.inc(
            v.scalar_tensor_tensor(wq_ap, a[:, 0:wraw_ap.shape[1]], 0.5,
                                   sgn[:, 0:wraw_ap.shape[1]], ALU.is_gt,
                                   ALU.mult), "sV")
    prog._tq_free = prog._tq2
    return prog._tq2


def emit_l2_ffn1(prog, aps, cst, xnT, h1T_hbm):
    block = prog.block
    scell, stok = emit_ternary_scale(prog, aps, cst, aps["L2_w1T"], 2048, 8192)
    col2, ctok = emit_quant_col(prog, cst, scell, stok, 1.0)
    rs_col, s_col = col2[:, 0:1], col2[:, 1:2]

    b1row = prog.sb("b1row", (1, 8192), F32)
    b1b = prog.sb("b1b", (128, 8192), BF16)

    @block.sync
    def _(s):
        prog._b1l = prog.dma(s, "qS", b1row[:], aps["L2_b1"])
    b1tok = emit_bcast_row(prog, cst, b1row, 8192, b1b, prog._b1l)

    wraw = prog.sb("wraw", (128, 8192), F32)
    wq = RR([prog.sb("wq1", (128, 8192), BF16) for _ in range(2)])
    qa = prog.sb("qa", (128, 512), F32)
    qs = prog.sb("qsn", (128, 512), BF16)
    strips = [prog.sb("h1s", (128, T), BF16) for _ in range(4)]
    striptok = [None]
    hp = RR([(prog.sb("h1p", (128, 512), F32), prog.sb("h1pb", (128, 512), BF16))
             for _ in range(2)])
    wraw_free = [None]

    for fb in range(16):
        @block.sync
        def _(s, fb=fb):
            prog.wait(s, wraw_free[0])
            for k in range(16):
                prog._wl1 = prog.dma(s, "qS", wraw[:, 512 * k:512 * (k + 1)],
                                     aps["L2_w1T"][128 * k:128 * (k + 1),
                                                   512 * fb:512 * (fb + 1)])

        wqi, wq_, wqtok = wq.nxt()
        qtoks = []
        for k in range(16):
            tk = emit_ternary_quant_piece(
                prog, cst, wraw[:, 512 * k:512 * (k + 1)],
                wq_[:, 512 * k:512 * (k + 1)], rs_col,
                _merge(prog._wl1, ctok, wqtok), (qa, qs))
            qtoks.append(tk)
        wraw_free[0] = _merge(qtoks)

        for ti in range(NT):
            bi, bank, ftok = prog.psum_pool.acquire()

            @block.tensor
            def _(t, ti=ti, wq_=wq_, bank=bank, ftok=ftok, qtoks=qtoks):
                prog.wait(t, ftok, xnT.ready(), qtoks)
                for k in range(16):
                    mm = t.matmul(bank[:, 0:512],
                                  xnT.blk(k, 128 * ti, 128 * (ti + 1)),
                                  wq_[:, 512 * k:512 * (k + 1)],
                                  start=(k == 0), stop=(k == 15))
                prog._m1 = prog.inc(mm, "sP")

            pi, (pf, pb), pftok = hp.nxt()

            @block.vector
            def _(v, fb=fb, bank=bank, pf=pf, pftok=pftok):
                prog.wait(v, prog._m1, pftok, b1tok, ctok)
                prog._e1 = prog.inc(
                    v.scalar_tensor_tensor(pf[:], bank[:, 0:512], 0.0,
                                           b1b[:, 512 * fb:512 * (fb + 1)],
                                           ALU.add, ALU.add), "sV")

            @block.scalar
            def _(s, pf=pf, pb=pb, bi=bi):
                prog.wait(s, prog._e1)
                tk = prog.inc(s.activation(out=pb[:], in_=pf[:], func=AF.Gelu),
                              "sS")
                prog.psum_pool.release(bi, tk)
                prog._g1 = tk

            pairs = [(pb[:, 128 * j:128 * (j + 1)],
                      strips[j][:, 128 * ti:128 * (ti + 1)],
                      _merge(prog._g1, striptok[0])) for j in range(4)]
            ttoks = emit_transposes(prog, cst, pairs)
            hp.done(pi, ttoks)

        @block.sync
        def _(s, fb=fb, ttoks=ttoks):
            prog.wait(s, ttoks)
            last = None
            for j in range(4):
                r = 4 * fb + j
                last = prog.dma(
                    s, "qS",
                    h1T_hbm[:, :, 128 * r:128 * (r + 1)].rearrange(
                        "t p c -> p t c"),
                    strips[j][:].rearrange("p (t c) -> p t c", c=128))
            striptok[0] = last
        wq.done(wqi, prog._m1)


def emit_l2_ffn2(prog, aps, cst, h1T_hbm, g_hbm):
    block = prog.block
    scell, stok = emit_ternary_scale(prog, aps, cst, aps["L2_w2T"], 8192, 2048)
    col2, ctok = emit_quant_col(prog, cst, scell, stok, 1.0)
    rs_col, s_col = col2[:, 0:1], col2[:, 1:2]

    b2row = prog.sb("b2row", (1, 2048), F32)
    b2b = prog.sb("b2b", (128, 2048), F32)

    @block.sync
    def _(s):
        prog._b2l = prog.dma(s, "qS", b2row[:], aps["L2_b2"])
    b2tok = emit_bcast_row(prog, cst, b2row, 2048, b2b, prog._b2l)

    wdq = prog.sb("wdq", (128, 64 * 512), BF16)
    wpiece = RR([prog.sb("wp2", (128, 512), F32) for _ in range(2)])
    qa = prog.sb("qa2", (128, 512), F32)
    qs = prog.sb("qsn2", (128, 512), BF16)
    hst = RR([prog.sb("hst2", (128, 8192), BF16) for _ in range(2)])
    gp = RR([(prog.sb("gp2", (128, 512), F32), prog.sb("gp2b", (128, 512), F32))
             for _ in range(2)])
    wdq_free = [None]

    for fbd in range(4):
        qtoks = []
        for ic in range(64):
            wi, wp, wptok = wpiece.nxt()

            @block.sync
            def _(s, ic=ic, fbd=fbd, wp=wp, wptok=wptok):
                prog.wait(s, wptok)
                prog._wl2 = prog.dma(s, "qS", wp[:],
                                     aps["L2_w2T"][128 * ic:128 * (ic + 1),
                                                   512 * fbd:512 * (fbd + 1)])

            tk = emit_ternary_quant_piece(
                prog, cst, wp[:], wdq[:, 512 * ic:512 * (ic + 1)], rs_col,
                _merge(prog._wl2, ctok, wdq_free[0]), (qa, qs))
            qtoks.append(tk)
            wpiece.done(wi, tk)

        for ti in range(NT):
            hi_, hst_, hftok = hst.nxt()

            @block.sync
            def _(s, ti=ti, hst_=hst_, hftok=hftok):
                prog.wait(s, hftok)
                prog._hl2 = prog.dma(s, "qS", hst_[:], h1T_hbm[ti])

            bi, bank, ftok = prog.psum_pool.acquire()

            @block.tensor
            def _(t, hst_=hst_, bank=bank, ftok=ftok, qtoks=qtoks):
                prog.wait(t, prog._hl2, ftok, qtoks)
                for ic in range(64):
                    mm = t.matmul(bank[:, 0:512],
                                  hst_[:, 128 * ic:128 * (ic + 1)],
                                  wdq[:, 512 * ic:512 * (ic + 1)],
                                  start=(ic == 0), stop=(ic == 63))
                prog._m2 = prog.inc(mm, "sP")
            hst.done(hi_, prog._m2)

            gi_, (gf, gb), gftok = gp.nxt()

            @block.vector
            def _(v, fbd=fbd, bank=bank, gf=gf, gftok=gftok):
                prog.wait(v, prog._m2, gftok, b2tok, ctok)
                prog._e2 = prog.inc(
                    v.scalar_tensor_tensor(gf[:], bank[:, 0:512], 0.0,
                                           b2b[:, 512 * fbd:512 * (fbd + 1)],
                                           ALU.add, ALU.add), "sV")

            @block.scalar
            def _(s, gf=gf, gb=gb, bi=bi):
                prog.wait(s, prog._e2)
                tk = prog.inc(s.activation(out=gb[:], in_=gf[:], func=AF.Silu),
                              "sS")
                prog.psum_pool.release(bi, tk)
                prog._g2 = tk

            @block.sync
            def _(s, ti=ti, fbd=fbd, gb=gb, gi_=gi_):
                prog.wait(s, prog._g2)
                gp.done(gi_, prog.dma(
                    s, "qS", g_hbm[128 * ti:128 * (ti + 1),
                                   512 * fbd:512 * (fbd + 1)], gb[:]))
        wdq_free[0] = prog._m2


def stage_layer2(inputs):
    a = {}
    a["L2_wq"] = _f32(inputs["L2_attn_wq"])
    a["L2_wk"] = _f32(inputs["L2_attn_wk"])
    a["L2_wv"] = _f32(inputs["L2_attn_wv"])
    a["L2_wo"] = _f32(inputs["L2_attn_wo"])
    a["L2_w1T"] = _f32(inputs["L2_ffn_w1"]).T.copy()
    a["L2_b1"] = _f32(inputs["L2_ffn_b1"]).reshape(1, 8192).copy()
    a["L2_w2T"] = _f32(inputs["L2_ffn_w2"]).T.copy()
    a["L2_b2"] = _f32(inputs["L2_ffn_b2"]).reshape(1, 2048).copy()
    inv = 1.0 / 10000.0 ** (np.arange(0, 128, 2, dtype=np.float64) / 128)
    t = np.arange(T, dtype=np.float64)
    fr = np.outer(t, inv)
    emb = np.concatenate([fr, fr], 1)
    a["C_l2cosT"] = np.cos(emb).T.astype(bf).copy()
    sinT = np.sin(emb).T
    sinT[0:64] = -sinT[0:64]
    a["C_l2sinT"] = sinT.astype(bf).copy()
    return a


LAYER_STAGERS[2] = stage_layer2
LAYER_EMITTERS[2] = emit_layer2



def emit_layer3(prog, aps, cst, x_in, x_out):
    block = prog.block
    u = prog.uid()
    kind = "ExternalOutput" if DBG else "Internal"
    s_hbm = prog.dram("dbg_s" if DBG else f"l3s_{u}", (T, D), F32, kind=kind)
    g_hbm = prog.dram("dbg_g" if DBG else f"l3g_{u}", (T, D), F32, kind=kind)
    attn_hbm = prog.dram("dbg_at" if DBG else f"l3at_{u}", (NT, 128, D), BF16,
                         kind=kind)
    h1_hbm = prog.dram(f"l3h1_{u}", (NT, 128, 8192), BF16)
    hqT_hbm = prog.dram(f"l3hqT_{u}", (NT, 128, 8192), BF16)

    with prog.scope():
        xnT = FM(prog, "xnT", D, T, BF16)
        emit_ln(prog, aps, cst, x_in, xnT)
        with prog.scope():
            emit_l3_attn(prog, aps, cst, xnT, attn_hbm)
    with prog.scope():
        emit_l3_rms_out(prog, aps, cst, attn_hbm, s_hbm)
    with prog.scope():
        sxr_cols = prog.sb("sxr", (128, NT), F32)
        shr_cols = prog.sb("shr", (128, NT), F32)
        with prog.scope():
            emit_l3_ffn1(prog, aps, cst, x_in, h1_hbm, sxr_cols)
        with prog.scope():
            emit_l3_hquant(prog, aps, cst, h1_hbm, hqT_hbm, shr_cols)
        with prog.scope():
            emit_l3_ffn2(prog, aps, cst, hqT_hbm, g_hbm, shr_cols)
    emit_combine(prog, cst, x_in, s_hbm, g_hbm, x_out)


def emit_l3_attn(prog, aps, cst, xnT, attn_hbm):
    block = prog.block
    H = 16
    import math
    n = H

    def pow2(nn):
        start = 2.0 ** (-2.0 ** (-(math.log2(nn) - 3)))
        return [start * start ** i for i in range(nn)]
    slopes = np.array(pow2(n), dtype=np.float64)
    r = np.exp(-slopes)
    rC = r ** 128.0

    prog.wbufs_s = RR([prog.sb("wbs", (128, 2048), BF16) for _ in range(1)])
    cosT = prog.sb("cosT3", (128, T), BF16)
    sinTs = prog.sb("sinT3", (128, T), BF16)
    prog._rope_tmp = prog.sb("rtmp3", (128, T), BF16)
    qdr1 = prog.sb("qdr1", (1, T), F32)
    kdcol = prog.sb("kdcol", (128, 16), F32)
    dmask = prog.sb("dmask", (128, 16 * 128), BF16)
    vtok = FM(prog, "vtok", T, D, BF16)

    @block.sync
    def _(s):
        prog.dma(s, "qS", cosT[:], aps["C_l3cosT"])
        prog.dma(s, "qS", sinTs[:], aps["C_l3sinTs"])
        prog.dma(s, "qS", kdcol[:], aps["C_l3kd"])
        prog._cl3 = prog.dma(s, "qS", dmask[:], aps["C_l3dmask"])

    fmtmp = prog.sb("fmtmp3", (128, T), BF16)
    for hh in range(16):
        vtk = emit_fm_block(prog, cst, aps["L3_Wv"], 128 * hh, xnT, fmtmp,
                            dst_free=getattr(prog, "_vtr3", None))
        pairs = [(fmtmp[:, 128 * i:128 * (i + 1)],
                  vtok.blk(i, 128 * hh, 128 * (hh + 1)), vtk)
                 for i in range(NT)]
        tt = emit_transposes(prog, cst, pairs)
        prog._vtr3 = tt
        vtok.wrote(tt)

    qraw = prog.sb("qraw3", (128, T), BF16)
    qrot = prog.sb("qrot3", (128, T), BF16)
    qtl = prog.sb("qtl3", (128, T), BF16)
    krot = prog.sb("krot3", (128, T), BF16)
    ktok = prog.sb("ktok3", (128, T), BF16)
    qd_b = prog.sb("qdb3", (128, T), F32)
    kv_sb = prog.sb("kv3", (128, 128), F32)
    kvb = prog.sb("kvb3", (128, 128), BF16)
    P_sb = prog.sb("P3", (128, 128), BF16)
    PT_sb = prog.sb("PT3", (128, 128), BF16)
    strip = prog.sb("strip3", (128, T), BF16)
    prog._l3f = None

    for h in range(H):
        qtk = emit_fm_block(prog, cst, aps["L3_Wq"], 128 * h, xnT, qraw,
                            dst_free=prog._l3f)
        rtk = emit_rope_fm(prog, cst, qraw, qrot, cosT, sinTs,
                           _merge(qtk, prog._cl3, prog._l3f), "l3")

        @block.sync
        def _(s, h=h):
            prog.wait(s, getattr(prog, "_qdb_used", None))
            prog._qdl = prog.dma(s, "qS", qdr1[:],
                                 aps["C_l3qdrow"][h:h + 1, :])
        qdtok = emit_bcast_row(prog, cst, qdr1, T, qd_b,
                               _merge(prog._qdl, prog._l3f))
        prog._qdb_used = qdtok

        @block.vector
        def _(v):
            prog.wait(v, rtk, qdtok)
            prog._qt3 = prog.inc(
                v.tensor_tensor(qtl[:], qrot[:], qd_b[:], ALU.mult), "sV")

        ktk = emit_fm_block(prog, cst, aps["L3_Wk"], 128 * h, xnT, qraw,
                            dst_free=_merge(prog._qt3, rtk))
        krtk = emit_rope_fm(prog, cst, qraw, krot, cosT, sinTs,
                            _merge(ktk, prog._cl3, getattr(prog, "_kt3", None),
                                   prog._l3f), "l3")
        pairs = [(krot[:, 128 * i:128 * (i + 1)], ktok[:, 128 * i:128 * (i + 1)],
                  _merge(krtk, getattr(prog, "_kt3", None))) for i in range(NT)]
        kttoks = emit_transposes(prog, cst, pairs)

        @block.vector
        def _(v, h=h):
            prog.wait(v, kttoks)
            prog._kt3 = prog.inc(
                v.tensor_single_scalar(ktok[:], ktok[:], kdcol[:, h:h + 1],
                                       ALU.mult), "sV")

        @block.vector
        def _(v):
            prog.wait(v, getattr(prog, "_kv_used", None))
            prog._kv0 = prog.inc(v.memset(kv_sb[:], 0.0), "sV")

        for c in range(NT):
            @block.vector
            def _(v):
                prog.wait(v, prog._kv0 if c == 0 else prog._kvup)
                prog._kvb = prog.inc(v.tensor_copy(out=kvb[:], in_=kv_sb[:]),
                                     "sV")

            bi_y, bank_y, ftok_y = prog.psum_pool.acquire()
            bi_s, bank_s, ftok_s = prog.psum_pool.acquire()

            @block.tensor
            def _(t, c=c, h=h, bank_y=bank_y, bank_s=bank_s, ftok_y=ftok_y,
                  ftok_s=ftok_s):
                prog.wait(t, prog._kvb, prog._qt3, ftok_y, ftok_s)
                t.matmul(bank_y[:, 0:128], qtl[:, 128 * c:128 * (c + 1)],
                         kvb[:], start=True, stop=False)
                prog._s3 = prog.inc(
                    t.matmul(bank_s[:, 0:128], qrot[:, 128 * c:128 * (c + 1)],
                             krot[:, 128 * c:128 * (c + 1)],
                             start=True, stop=True), "sP")

            @block.vector
            def _(v, h=h, bank_s=bank_s, bi_s=bi_s):
                prog.wait(v, prog._s3, getattr(prog, "_pt3", None))
                tk = prog.inc(v.tensor_tensor(P_sb[:], bank_s[:, 0:128],
                                              dmask[:, 128 * h:128 * (h + 1)],
                                              ALU.mult), "sV")
                prog.psum_pool.release(bi_s, tk)
                prog._p3 = tk

            pt = emit_transposes(prog, cst, [(P_sb[:], PT_sb[:],
                                              _merge(prog._p3,
                                                     getattr(prog, "_pt3", None)))])

            bi_u, bank_u, ftok_u = prog.psum_pool.acquire()

            @block.tensor
            def _(t, c=c, h=h, bank_y=bank_y, bank_u=bank_u, ftok_u=ftok_u,
                  pt=pt):
                prog.wait(t, pt, vtok.ready(), prog._kt3, ftok_u)
                prog._y3 = prog.inc(
                    t.matmul(bank_y[:, 0:128], PT_sb[:],
                             vtok.blk(c, 128 * h, 128 * (h + 1)),
                             start=False, stop=True), "sP")
                prog._su3 = prog.inc(
                    t.matmul(bank_u[:, 0:128], ktok[:, 128 * c:128 * (c + 1)],
                             vtok.blk(c, 128 * h, 128 * (h + 1)),
                             start=True, stop=True), "sP")
            prog._pt3 = prog._y3

            @block.vector
            def _(v, c=c, bank_y=bank_y, bi_y=bi_y):
                prog.wait(v, prog._y3, getattr(prog, "_st3", None))
                tk = prog.inc(v.tensor_copy(out=strip[:, 128 * c:128 * (c + 1)],
                                            in_=bank_y[:, 0:128]), "sV")
                prog.psum_pool.release(bi_y, tk)
                prog._ye3 = tk

            @block.vector
            def _(v, h=h, bank_u=bank_u, bi_u=bi_u):
                prog.wait(v, prog._su3, prog._kvb)
                tk = prog.inc(
                    v.scalar_tensor_tensor(kv_sb[:], kv_sb[:], float(rC[h]),
                                           bank_u[:, 0:128], ALU.mult, ALU.add),
                    "sV")
                prog.psum_pool.release(bi_u, tk)
                prog._kvup = tk
        prog._kv_used = prog._kvb

        @block.sync
        def _(s, h=h):
            prog.wait(s, prog._ye3)
            prog._st3 = prog.dma(
                s, "qS",
                attn_hbm[:, :, 128 * h:128 * (h + 1)].rearrange("t p c -> p t c"),
                strip[:].rearrange("p (t c) -> p t c", c=128))
        prog._l3f = _merge(prog._s3, prog._qt3)


def emit_l3_rms_out(prog, aps, cst, attn_hbm, s_hbm):
    block = prog.block
    wo = prog.sb("wo3", (128, 16 * 2048), BF16)

    @block.gpsimd
    def _(g):
        for k in range(16):
            prog._wol = prog.dma(g, "qG", wo[:, 2048 * k:2048 * (k + 1)],
                                 aps["L3_Wo"][128 * k:128 * (k + 1), :])

    rmsrow = prog.sb("rmsrow", (1, D), F32)
    rmsb = prog.sb("rmsb", (128, D), F32)

    @block.sync
    def _(s):
        prog._rml = prog.dma(s, "qS", rmsrow[:], aps["L3_rmsw"])
    rmtok = emit_bcast_row(prog, cst, rmsrow, D, rmsb, prog._rml)

    at = RR([prog.sb("at3", (128, D), BF16) for _ in range(2)])
    atn = prog.sb("atn3", (128, D), BF16)
    anT = RR([prog.sb("anT3", (128, D), BF16) for _ in range(2)])
    cols = prog.sb("c3", (128, 4), F32)
    spieces = RR([prog.sb("sp3", (128, 512), F32) for _ in range(3)])

    for ti in range(NT):
        ai, a_, aftok = at.nxt()

        @block.sync
        def _(s, ti=ti, a_=a_, aftok=aftok):
            prog.wait(s, aftok)
            prog._al = prog.dma(s, "qS", a_[:], attn_hbm[ti])

        @block.vector
        def _(v, a_=a_):
            prog.wait(v, prog._al, getattr(prog, "_ann", None))
            v.tensor_tensor(atn[:], a_[:], a_[:], ALU.mult)
            prog._ms = prog.inc(v.reduce_sum(cols[:, 0:1], atn[:], axis=AX.X),
                                "sV")

        @block.scalar
        def _(s):
            prog.wait(s, prog._ms)
            s.activation(out=cols[:, 1:2], in_=cols[:, 0:1], func=AF.Ln,
                         scale=1.0 / D, bias=cst["c_eps1e5"][:, 0:1])
            prog._rs = prog.inc(s.activation(out=cols[:, 1:2], in_=cols[:, 1:2],
                                             func=AF.Exp, scale=-0.5), "sS")

        @block.vector
        def _(v, a_=a_, ai=ai):
            prog.wait(v, prog._rs, rmtok)
            v.tensor_single_scalar(atn[:], a_[:], cols[:, 1:2], ALU.mult)
            tk = prog.inc(v.tensor_tensor(atn[:], atn[:], rmsb[:], ALU.mult),
                          "sV")
            at.done(ai, tk)
            prog._ann = tk

        ni, anT_, nftok = anT.nxt()
        pairs = [(atn[:, 128 * c:128 * (c + 1)], anT_[:, 128 * c:128 * (c + 1)],
                  _merge(prog._ann, nftok)) for c in range(16)]
        ttoks = emit_transposes(prog, cst, pairs)

        for f0 in range(0, D, 512):
            bi, bank, ftok = prog.psum_pool.acquire()

            @block.tensor
            def _(t, ti=ti, f0=f0, anT_=anT_, bank=bank, ftok=ftok,
                  ttoks=ttoks):
                prog.wait(t, ttoks, prog._wol, ftok)
                for dc in range(16):
                    mm = t.matmul(bank[:, 0:512],
                                  anT_[:, 128 * dc:128 * (dc + 1)],
                                  wo[:, 2048 * dc + f0:2048 * dc + f0 + 512],
                                  start=(dc == 0), stop=(dc == 15))
                prog._om = prog.inc(mm, "sP")

            pi, piece, pftok = spieces.nxt()

            @block.vector
            def _(v, bank=bank, bi=bi, piece=piece, pftok=pftok):
                prog.wait(v, prog._om, pftok)
                tk = prog.inc(v.tensor_copy(out=piece[:], in_=bank[:, 0:512]),
                              "sV")
                prog.psum_pool.release(bi, tk)
                prog._se3 = tk

            @block.sync
            def _(s, ti=ti, f0=f0, piece=piece, pi=pi):
                prog.wait(s, prog._se3)
                spieces.done(pi, prog.dma(
                    s, "qS", s_hbm[128 * ti:128 * (ti + 1), f0:f0 + 512],
                    piece[:]))
        anT.done(ni, prog._om)


def emit_l3_ffn1(prog, aps, cst, x_in, h1_hbm, sxr_cols):
    block = prog.block
    ms_cell, mstok = emit_ternary_scale(prog, aps, cst, aps["L3_w1T"],
                                        2048, 8192, kind="rms")
    scell = prog.sb("s1c3", (1, 2), F32)

    @block.vector
    def _(v):
        prog.wait(v, mstok)
        prog._mx3 = prog.inc(v.tensor_single_scalar(scell[0:1, 0:1],
                                                    ms_cell[0:1, 0:1], 1e-5,
                                                    ALU.max), "sV")

    @block.scalar
    def _(s):
        prog.wait(s, prog._mx3)
        prog._sq3 = prog.inc(s.activation(out=scell[0:1, 0:1],
                                          in_=scell[0:1, 0:1], func=AF.Sqrt),
                             "sS")
    col2, ctok = emit_quant_col(prog, cst, scell, prog._sq3, 1.0)
    rs_col, s_col = col2[:, 0:1], col2[:, 1:2]

    xqT = FM(prog, "xqT", D, T, BF16)
    lnb = RR([(prog.sb("l3x", (128, D), F32), prog.sb("l3q", (128, D), BF16),
               prog.sb("l3c", (128, 6), F32)) for _ in range(2)])
    for i in range(NT):
        bi_, (xa, xq, cols), ftok = lnb.nxt()

        @block.sync
        def _(s, i=i, xa=xa, ftok=ftok):
            prog.wait(s, ftok)
            prog._xl3 = prog.dma(s, "qS", xa[:],
                                 x_in[128 * i:128 * (i + 1), :])

        @block.vector
        def _(v, xa=xa, cols=cols):
            prog.wait(v, prog._xl3)
            v.reduce_sum(cols[:, 0:1], xa[:], axis=AX.X)
            prog._m3 = prog.inc(v.tensor_scalar_mul(cols[:, 0:1], cols[:, 0:1],
                                                    1.0 / D), "sV")

        @block.scalar
        def _(s, xa=xa, xq=xq, cols=cols):
            prog.wait(s, prog._m3)
            prog._s23 = prog.inc(
                s.activation(out=xq[:], in_=xa[:], func=AF.Square,
                             accum_out=cols[:, 1:2]), "sS")

        @block.vector
        def _(v, cols=cols):
            prog.wait(v, prog._s23)
            v.tensor_scalar_mul(cols[:, 1:2], cols[:, 1:2], 1.0 / D)
            v.tensor_tensor(cols[:, 2:3], cols[:, 0:1], cols[:, 0:1], ALU.mult)
            prog._v3 = prog.inc(v.tensor_tensor(cols[:, 1:2], cols[:, 1:2],
                                                cols[:, 2:3], ALU.subtract),
                                "sV")

        @block.scalar
        def _(s, cols=cols):
            prog.wait(s, prog._v3)
            s.activation(out=cols[:, 2:3], in_=cols[:, 1:2], func=AF.Ln,
                         bias=cst["c_eps1e5"][:, 0:1])
            prog._r3 = prog.inc(s.activation(out=cols[:, 2:3], in_=cols[:, 2:3],
                                             func=AF.Exp, scale=-0.5), "sS")

        @block.vector
        def _(v, i=i, xa=xa, xq=xq, cols=cols):
            prog.wait(v, prog._r3)
            v.tensor_scalar(xa[:], xa[:], cols[:, 0:1], cols[:, 2:3],
                            ALU.subtract, ALU.mult)
            v.reduce_max(cols[:, 3:4], xa[:], axis=AX.X,
                         apply_absolute_value=True)
            v.tensor_single_scalar(cols[:, 3:4], cols[:, 3:4], 1e-5, ALU.max)
            v.reciprocal(cols[:, 4:5], cols[:, 3:4])
            v.tensor_scalar_mul(cols[:, 4:5], cols[:, 4:5], 127.0)
            v.tensor_scalar_mul(sxr_cols[:, i:i + 1], cols[:, 3:4], 1.0 / 127.0)
            v.tensor_scalar(xa[:], xa[:], cols[:, 4:5], 12582912.0,
                            ALU.mult, ALU.add)
            v.tensor_scalar(xa[:], xa[:], 12582912.0, -128.0,
                            ALU.subtract, ALU.max)
            prog._xq3 = prog.inc(
                v.tensor_single_scalar(xq[:], xa[:], 127.0, ALU.min), "sV")

        pairs = [(xq[:, 128 * c:128 * (c + 1)],
                  xqT.blk(c, 128 * i, 128 * (i + 1)), prog._xq3)
                 for c in range(16)]
        ttoks = emit_transposes(prog, cst, pairs)
        xqT.wrote(ttoks)
        lnb.done(bi_, ttoks)

    wraw = prog.sb("wraw3", (128, 8192), F32)
    wq = RR([prog.sb("wq3", (128, 8192), BF16) for _ in range(2)])
    qa = prog.sb("qa3", (128, 512), F32)
    qs = prog.sb("qsn3", (128, 512), BF16)
    hp = RR([(prog.sb("h3p", (128, 512), F32), prog.sb("h3pb", (128, 512), BF16))
             for _ in range(2)])
    wraw_free = [None]
    for fb in range(16):
        @block.sync
        def _(s, fb=fb):
            prog.wait(s, wraw_free[0])
            for k in range(16):
                prog._wl3 = prog.dma(s, "qS", wraw[:, 512 * k:512 * (k + 1)],
                                     aps["L3_w1T"][128 * k:128 * (k + 1),
                                                   512 * fb:512 * (fb + 1)])

        wqi, wq_, wqtok = wq.nxt()
        qtoks = []
        for k in range(16):
            tk = emit_ternary_quant_piece(
                prog, cst, wraw[:, 512 * k:512 * (k + 1)],
                wq_[:, 512 * k:512 * (k + 1)], rs_col,
                _merge(prog._wl3, ctok, wqtok), (qa, qs))
            qtoks.append(tk)
        wraw_free[0] = _merge(qtoks)

        for ti in range(NT):
            bi, bank, ftok = prog.psum_pool.acquire()

            @block.tensor
            def _(t, ti=ti, wq_=wq_, bank=bank, ftok=ftok, qtoks=qtoks):
                prog.wait(t, ftok, xqT.ready(), qtoks)
                for k in range(16):
                    mm = t.matmul(bank[:, 0:512],
                                  xqT.blk(k, 128 * ti, 128 * (ti + 1)),
                                  wq_[:, 512 * k:512 * (k + 1)],
                                  start=(k == 0), stop=(k == 15))
                prog._m13 = prog.inc(mm, "sP")

            pi, (pf, pb), pftok = hp.nxt()

            @block.vector
            def _(v, ti=ti, bank=bank, pf=pf, pftok=pftok):
                prog.wait(v, prog._m13, pftok, ctok)
                prog._e13 = prog.inc(
                    v.tensor_scalar(pf[:], bank[:, 0:512],
                                    sxr_cols[:, ti:ti + 1], s_col[:, 0:1],
                                    ALU.mult, ALU.mult), "sV")

            @block.scalar
            def _(s, pf=pf, pb=pb, bi=bi):
                prog.wait(s, prog._e13)
                tk = prog.inc(s.activation(out=pb[:], in_=pf[:], func=AF.Gelu),
                              "sS")
                prog.psum_pool.release(bi, tk)
                prog._g13 = tk

            @block.sync
            def _(s, ti=ti, fb=fb, pb=pb, pi=pi):
                prog.wait(s, prog._g13)
                hp.done(pi, prog.dma(
                    s, "qS", h1_hbm[ti][:, 512 * fb:512 * (fb + 1)], pb[:]))
        wq.done(wqi, prog._m13)


def emit_l3_hquant(prog, aps, cst, h1_hbm, hqT_hbm, shr_cols):
    block = prog.block
    hl = RR([prog.sb("hq1", (128, 8192), BF16) for _ in range(2)])
    hf = prog.sb("hqf", (128, 8192), F32)
    hq = RR([prog.sb("hq2", (128, 8192), BF16) for _ in range(2)])
    hqT = RR([prog.sb("hqT", (128, 8192), BF16) for _ in range(2)])
    cols = prog.sb("hqc", (128, 2), F32)
    for ti in range(NT):
        li, h_, lftok = hl.nxt()

        @block.sync
        def _(s, ti=ti, h_=h_, lftok=lftok):
            prog.wait(s, lftok)
            prog._hl3 = prog.dma(s, "qS", h_[:], h1_hbm[ti])

        qi_, hq_, qftok = hq.nxt()

        @block.vector
        def _(v, ti=ti, h_=h_, hq_=hq_, qftok=qftok, li=li):
            prog.wait(v, prog._hl3, qftok, getattr(prog, "_hff", None))
            v.reduce_max(cols[:, 0:1], h_[:], axis=AX.X,
                         apply_absolute_value=True)
            v.tensor_single_scalar(cols[:, 0:1], cols[:, 0:1], 1e-5, ALU.max)
            v.reciprocal(cols[:, 1:2], cols[:, 0:1])
            v.tensor_scalar_mul(cols[:, 1:2], cols[:, 1:2], 127.0)
            v.tensor_scalar_mul(shr_cols[:, ti:ti + 1], cols[:, 0:1],
                                1.0 / 127.0)
            v.tensor_scalar(hf[:], h_[:], cols[:, 1:2], 12582912.0,
                            ALU.mult, ALU.add)
            v.tensor_scalar(hf[:], hf[:], 12582912.0, -128.0,
                            ALU.subtract, ALU.max)
            tk = prog.inc(v.tensor_single_scalar(hq_[:], hf[:], 127.0, ALU.min),
                          "sV")
            hl.done(li, tk)
            prog._hq3 = tk
            prog._hff = tk

        tti, hqT_, tftok = hqT.nxt()
        pairs = [(hq_[:, 128 * c:128 * (c + 1)], hqT_[:, 128 * c:128 * (c + 1)],
                  _merge(prog._hq3, tftok)) for c in range(64)]
        ttoks = emit_transposes(prog, cst, pairs)
        hq.done(qi_, ttoks)

        @block.sync
        def _(s, ti=ti, hqT_=hqT_, ttoks=ttoks):
            prog.wait(s, ttoks)
            prog._ht3 = prog.dma(s, "qS", hqT_hbm[ti], hqT_[:])
        hqT.done(tti, prog._ht3)


def emit_l3_ffn2(prog, aps, cst, hqT_hbm, g_hbm, shr_cols):
    block = prog.block
    ms_cell, mstok = emit_ternary_scale(prog, aps, cst, aps["L3_w2T"],
                                        8192, 2048, kind="rms")
    scell = prog.sb("s2c3", (1, 2), F32)

    @block.vector
    def _(v):
        prog.wait(v, mstok)
        prog._mx4 = prog.inc(v.tensor_single_scalar(scell[0:1, 0:1],
                                                    ms_cell[0:1, 0:1], 1e-5,
                                                    ALU.max), "sV")

    @block.scalar
    def _(s):
        prog.wait(s, prog._mx4)
        prog._sq4 = prog.inc(s.activation(out=scell[0:1, 0:1],
                                          in_=scell[0:1, 0:1], func=AF.Sqrt),
                             "sS")
    col2, ctok = emit_quant_col(prog, cst, scell, prog._sq4, 1.0)
    rs_col, s_col = col2[:, 0:1], col2[:, 1:2]

    wdq = prog.sb("wdq3", (128, 64 * 512), BF16)
    wpiece = RR([prog.sb("wp3", (128, 512), F32) for _ in range(2)])
    qa = prog.sb("qa4", (128, 512), F32)
    qs = prog.sb("qsn4", (128, 512), BF16)
    hst = RR([prog.sb("hst3", (128, 8192), BF16) for _ in range(2)])
    gp = RR([(prog.sb("gp3", (128, 512), F32), prog.sb("gp3b", (128, 512), F32))
             for _ in range(2)])
    wdq_free = [None]

    for fbd in range(4):
        qtoks = []
        for ic in range(64):
            wi, wp, wptok = wpiece.nxt()

            @block.sync
            def _(s, ic=ic, fbd=fbd, wp=wp, wptok=wptok):
                prog.wait(s, wptok)
                prog._wl4 = prog.dma(s, "qS", wp[:],
                                     aps["L3_w2T"][128 * ic:128 * (ic + 1),
                                                   512 * fbd:512 * (fbd + 1)])

            tk = emit_ternary_quant_piece(
                prog, cst, wp[:], wdq[:, 512 * ic:512 * (ic + 1)], rs_col,
                _merge(prog._wl4, ctok, wdq_free[0]), (qa, qs))
            qtoks.append(tk)
            wpiece.done(wi, tk)

        for ti in range(NT):
            hi_, hst_, hftok = hst.nxt()

            @block.sync
            def _(s, ti=ti, hst_=hst_, hftok=hftok):
                prog.wait(s, hftok)
                prog._hl4 = prog.dma(s, "qS", hst_[:], hqT_hbm[ti])

            bi, bank, ftok = prog.psum_pool.acquire()

            @block.tensor
            def _(t, hst_=hst_, bank=bank, ftok=ftok, qtoks=qtoks):
                prog.wait(t, prog._hl4, ftok, qtoks)
                for ic in range(64):
                    mm = t.matmul(bank[:, 0:512],
                                  hst_[:, 128 * ic:128 * (ic + 1)],
                                  wdq[:, 512 * ic:512 * (ic + 1)],
                                  start=(ic == 0), stop=(ic == 63))
                prog._m4 = prog.inc(mm, "sP")
            hst.done(hi_, prog._m4)

            gi_, (gf, gb), gftok = gp.nxt()

            @block.vector
            def _(v, ti=ti, bank=bank, gf=gf, gftok=gftok):
                prog.wait(v, prog._m4, gftok, ctok)
                prog._e4 = prog.inc(
                    v.tensor_scalar(gf[:], bank[:, 0:512],
                                    shr_cols[:, ti:ti + 1], s_col[:, 0:1],
                                    ALU.mult, ALU.mult), "sV")

            @block.scalar
            def _(s, gf=gf, gb=gb, bi=bi):
                prog.wait(s, prog._e4)
                tk = prog.inc(s.activation(out=gb[:], in_=gf[:], func=AF.Silu),
                              "sS")
                prog.psum_pool.release(bi, tk)
                prog._g4 = tk

            @block.sync
            def _(s, ti=ti, fbd=fbd, gb=gb, gi_=gi_):
                prog.wait(s, prog._g4)
                gp.done(gi_, prog.dma(
                    s, "qS", g_hbm[128 * ti:128 * (ti + 1),
                                   512 * fbd:512 * (fbd + 1)], gb[:]))
        wdq_free[0] = prog._m4


def stage_layer3(inputs):
    import math
    a = {}
    a["L3_Wq"] = _f32(inputs["L3_attn_Wq"])
    a["L3_Wk"] = _f32(inputs["L3_attn_Wk"])
    a["L3_Wv"] = _f32(inputs["L3_attn_Wv"])
    a["L3_Wo"] = _f32(inputs["L3_attn_Wo"])
    a["L3_rmsw"] = _f32(inputs["L3_attn_rms_w"]).reshape(1, 2048).copy()
    a["L3_w1T"] = _f32(inputs["L3_ffn_w1"]).T.copy()
    a["L3_w2T"] = _f32(inputs["L3_ffn_w2"]).T.copy()
    fr = (1.0 / 1024.0) ** np.linspace(0.0, 1.0, 32)
    fr = np.concatenate([fr, np.zeros(32)])
    t = np.arange(T, dtype=np.float64)
    th = np.outer(fr, t)
    cos = np.concatenate([np.cos(th), np.cos(th)], 0)
    sin = np.concatenate([np.sin(th), -np.sin(th)], 0)
    a["C_l3cosT"] = cos.astype(bf).copy()
    a["C_l3sinTs"] = sin.astype(bf).copy()
    H = 16

    def pow2(nn):
        start = 2.0 ** (-2.0 ** (-(math.log2(nn) - 3)))
        return [start * start ** i for i in range(nn)]
    slopes = np.array(pow2(H), dtype=np.float64)
    r = np.exp(-slopes)
    i = np.arange(128, dtype=np.float64)
    qd = r[:, None] ** (i[None, :] + 1.0)
    a["C_l3qdrow"] = np.tile(qd, (1, 16)).astype(np.float32).copy()
    kd = r[:, None] ** (127.0 - i[None, :])
    a["C_l3kd"] = kd.T.astype(np.float32).copy()
    diff = i[:, None] - i[None, :]
    dm = np.where(diff >= 0, r[:, None, None] ** np.where(diff >= 0, diff, 0.0),
                  0.0)
    a["C_l3dmask"] = dm.transpose(1, 0, 2).reshape(128, 16 * 128).astype(bf).copy()
    return a


LAYER_STAGERS[3] = stage_layer3
LAYER_EMITTERS[3] = emit_layer3



def emit_layer0(prog, aps, cst, x_in, x_out):
    block = prog.block
    u = prog.uid()
    kind = "ExternalOutput" if DBG else "Internal"
    s_hbm = prog.dram("dbg_s" if DBG else f"l0s_{u}", (T, D), F32, kind=kind)
    g_hbm = prog.dram("dbg_g" if DBG else f"l0g_{u}", (T, D), F32, kind=kind)
    qxT_hbm = prog.dram(f"l0qx_{u}", (NT, 128, D), BF16)
    base_hbm = prog.dram("dbg_base" if DBG else f"l0base_{u}", (NT, 128, 5632),
                         F32, kind=kind)
    qtT_hbm = prog.dram("dbg_qt" if DBG else f"l0qt_{u}", (NT, 128, 5632),
                        BF16, kind=kind)
    gsilu_hbm = prog.dram("dbg_gs" if DBG else f"l0gs_{u}", (NT, 128, D), BF16,
                          kind=kind)
    ygT_hbm = prog.dram("dbg_yg" if DBG else f"l0yg_{u}", (NT, 128, D), BF16,
                        kind=kind)
    wgqT_hbm = prog.dram(f"l0wgq_{u}", (D, 5632), BF16)
    wuqT_hbm = prog.dram(f"l0wuq_{u}", (D, 5632), BF16)
    wdqT_hbm = prog.dram("dbg_wdq" if DBG else f"l0wdq_{u}", (5632, D), BF16,
                         kind=kind)

    vkq_hbm = prog.dram("dbg_vkq" if DBG else f"l0vkq_{u}",
                        (NT, 24, 128, 128), BF16, kind=kind)
    with prog.scope():
        qsc_cols = prog.sb("qsc", (128, NT), F32)
        rmt_cols = prog.sb("rmt", (128, NT), F32)
        wsg_cols = prog.sb("wsgc", (128, 44), F32)
        wsu_cols = prog.sb("wsuc", (128, 44), F32)
        wsd_cols = prog.sb("wsdc", (128, 16), F32)
        dtT = prog.sb("dtT", (32, T), F32)
        dtvec = prog.sb("dtvec", (128, 8), F32)

        with prog.scope():
            emit_l0_wprep(prog, aps, cst, aps["L0_wg"], wgqT_hbm, wsg_cols,
                          44, perm=True)
            emit_l0_wprep(prog, aps, cst, aps["L0_wu"], wuqT_hbm, wsu_cols,
                          44, perm=True)
            emit_l0_wprep(prog, aps, cst, aps["L0_wd"], wdqT_hbm, wsd_cols,
                          16, perm=False)
        with prog.scope():
            xnT = FM(prog, "xnT", D, T, BF16)
            with prog.scope():
                emit_l0_ln_quant(prog, aps, cst, x_in, xnT, qxT_hbm, qsc_cols)
            with prog.scope():
                emit_l0_proj_g_dt(prog, aps, cst, xnT, gsilu_hbm, dtT, dtvec)
            with prog.scope():
                emit_l0_proj_vkq(prog, aps, cst, xnT, vkq_hbm)
        with prog.scope():
            emit_l0_gateup(prog, aps, cst, qxT_hbm, wgqT_hbm, wuqT_hbm,
                           wsg_cols, wsu_cols, qsc_cols, base_hbm, rmt_cols)
        with prog.scope():
            emit_l0_ssd(prog, aps, cst, vkq_hbm, dtT, gsilu_hbm, ygT_hbm)
        with prog.scope():
            emit_l0_outproj(prog, aps, cst, ygT_hbm, s_hbm)
        qscT_cols = prog.sb("qscT", (128, NT), F32)
        with prog.scope():
            emit_l0_qt(prog, aps, cst, base_hbm, rmt_cols, qtT_hbm, qscT_cols)
        with prog.scope():
            emit_l0_down(prog, aps, cst, qtT_hbm, wdqT_hbm, wsd_cols, qscT_cols,
                         g_hbm)
    emit_combine(prog, cst, x_in, s_hbm, g_hbm, x_out)


def emit_col_bcast(prog, cst, col_ap, dst_ap, in_tok):
    block = prog.block
    rowb = prog.sb("cbrow", (1, 128), F32)
    bi, bank, ftok = prog.psum_pool.acquire()

    @block.tensor
    def _(t):
        prog.wait(t, in_tok, ftok)
        prog._cb1 = prog.inc(t.matmul(bank[0:1, 0:128], col_ap, cst["idf"][:],
                                      start=True, stop=True), "sP")

    @block.vector
    def _(v):
        prog.wait(v, prog._cb1)
        prog._cb2 = prog.inc(v.tensor_copy(out=rowb[:], in_=bank[0:1, 0:128]),
                             "sV")

    bi2, bank2, ftok2 = prog.psum_pool.acquire()

    @block.tensor
    def _(t):
        prog.wait(t, prog._cb2, ftok2)
        prog._cb3 = prog.inc(t.matmul(bank2[:, 0:128], cst["ones_row_f"][0:1, :],
                                      rowb[0:1, :], start=True, stop=True), "sP")

    @block.vector
    def _(v):
        prog.wait(v, prog._cb3)
        tk = prog.inc(v.tensor_copy(out=dst_ap, in_=bank2[:, 0:128]), "sV")
        prog.psum_pool.release(bi2, tk)
        prog.psum_pool.release(bi, tk)
        prog._cb4 = tk
    return prog._cb4


def emit_l0_ln_quant(prog, aps, cst, x_in, xnT, qxT_hbm, qsc_cols):
    block = prog.block
    q8 = prog.sb("q8", (128, D), FP8)
    qxb = prog.sb("qxb", (128, D), BF16)
    qcol = prog.sb("qcol", (128, 2), F32)
    qxt = RR([prog.sb("qxt", (128, D), BF16) for _ in range(2)])
    state = {"dmatok": None}

    def hook(i, xb2, tok):
        @block.vector
        def _(v):
            prog.wait(v, tok, getattr(prog, "_q8f", None))
            v.reduce_max(qcol[:, 0:1], xb2[:], axis=AX.X,
                         apply_absolute_value=True)
            v.tensor_scalar_mul(qsc_cols[:, i:i + 1], qcol[:, 0:1], 1.0 / 112.0)
            v.reciprocal(qcol[:, 1:2], qcol[:, 0:1])
            v.tensor_scalar_mul(qcol[:, 1:2], qcol[:, 1:2], 224.0)
            v.tensor_single_scalar(q8[:], xb2[:], qcol[:, 1:2], ALU.mult)
            prog._q8f = prog.inc(v.tensor_copy(out=qxb[:], in_=q8[:]), "sV")

        ti_, qxt_, tftok = qxt.nxt()
        pairs = [(qxb[:, 128 * c:128 * (c + 1)], qxt_[:, 128 * c:128 * (c + 1)],
                  _merge(prog._q8f, tftok)) for c in range(16)]
        ttoks = emit_transposes(prog, cst, pairs)

        @block.sync
        def _(s):
            prog.wait(s, ttoks)
            prog._qxs = prog.dma(s, "qS", qxT_hbm[i], qxt_[:])
        qxt.done(ti_, prog._qxs)
        return _merge(prog._q8f)

    emit_ln(prog, aps, cst, x_in, xnT, tile_hook=hook)


def emit_l0_wprep(prog, aps, cst, w_hbm, wqT_hbm, ws_cols, nstrip, perm):
    block = prog.block
    K_ = w_hbm.shape[1]
    NKc = K_ // 128
    wstrip = RR([prog.sb("wps", (128, K_), F32) for _ in range(2)])
    q8 = prog.sb("wp8", (128, K_), FP8)
    q8b = prog.sb("wp8b", (128, K_), BF16)
    wcol = prog.sb("wpc", (128, 2), F32)
    wsc = prog.sb("wpsc", (128, 1), F32)
    tb = RR([prog.sb("wptb", (128, K_), BF16) for _ in range(2)])
    idxc = prog.sb("wpidx", (128, 1), I32)

    for s_ in range(nstrip):
        wi, ws, wtok = wstrip.nxt()
        if perm:
            @block.sync
            def _(sy, s_=s_):
                prog.wait(sy, getattr(prog, "_wpg", None))
                prog._wpi = prog.dma(sy, "qS", idxc[:],
                                     aps["L0_didx"][128 * s_:128 * (s_ + 1), :])

            @block.gpsimd
            def _(g, ws=ws, wtok=wtok):
                prog.wait(g, prog._wpi, wtok)
                gi = g.indirect_dma_start(
                    out=ws[:], out_offset=None, in_=w_hbm,
                    in_offset=bass.IndirectOffsetOnAxis(idxc[:], 0))
                prog._wpl = prog.inc(gi, "qG", 16)
                prog._wpg = prog._wpl
        else:
            @block.sync
            def _(sy, s_=s_, ws=ws, wtok=wtok):
                prog.wait(sy, wtok)
                prog._wpl = prog.dma(sy, "qS", ws[:],
                                     w_hbm[128 * s_:128 * (s_ + 1), :])

        @block.vector
        def _(v, ws=ws, wi=wi, s_=s_):
            prog.wait(v, prog._wpl, getattr(prog, "_wp8f", None))
            v.reduce_max(wcol[:, 0:1], ws[:], axis=AX.X,
                         apply_absolute_value=True)
            v.tensor_scalar_mul(ws_cols[:, s_:s_ + 1], wcol[:, 0:1], 1.0 / 448.0)
            v.reciprocal(wcol[:, 1:2], wcol[:, 0:1])
            v.tensor_scalar_mul(wcol[:, 1:2], wcol[:, 1:2], 224.0)
            v.tensor_single_scalar(q8[:, 0:K_], ws[:], wcol[:, 1:2], ALU.mult)
            prog._wp8f = prog.inc(v.tensor_copy(out=q8b[:, 0:K_],
                                                in_=q8[:, 0:K_]), "sV")
            wstrip.done(wi, prog._wp8f)

        ti_, tb_, tftok = tb.nxt()
        pairs = [(q8b[:, 128 * c:128 * (c + 1)], tb_[:, 128 * c:128 * (c + 1)],
                  _merge(prog._wp8f, tftok)) for c in range(NKc)]
        ttoks = emit_transposes(prog, cst, pairs)

        @block.sync
        def _(sy, s_=s_, tb_=tb_, ttoks=ttoks):
            prog.wait(sy, ttoks)
            prog._wqs = prog.dma(
                sy, "qS",
                wqT_hbm[:, 128 * s_:128 * (s_ + 1)].rearrange(
                    "(kc p) c -> p kc c", p=128),
                tb_[:].rearrange("p (kc c) -> p kc c", c=128))
        tb.done(ti_, prog._wqs)


def emit_l0_gateup(prog, aps, cst, qxT_hbm, wgqT_hbm, wuqT_hbm, wsg_cols,
                   wsu_cols, qsc_cols, base_hbm, rmt_cols):
    block = prog.block
    I_ = 5632
    wg_b = RR([prog.sb("l0wg", (128, 8192), BF16) for _ in range(2)])
    wu_b = RR([prog.sb("l0wu", (128, 8192), BF16) for _ in range(2)])
    qx = RR([prog.sb("l0qx", (128, D), BF16) for _ in range(2)])
    wsgb = prog.sb("wsgb", (128, 512), F32)
    wsub = prog.sb("wsub", (128, 512), F32)
    gs = RR([prog.sb("l0gs", (128, 512), F32) for _ in range(2)])
    us = RR([prog.sb("l0us", (128, 512), F32) for _ in range(2)])
    bp = RR([prog.sb("l0bp", (128, 512), F32) for _ in range(3)])
    first_fb = [True]

    for f0 in range(0, I_, 512):
        f1 = min(I_, f0 + 512)
        fw = f1 - f0
        wgi, wg_, wgtok = wg_b.nxt()
        wui, wu_, wutok = wu_b.nxt()

        @block.gpsimd
        def _(g, f0=f0, f1=f1, wg_=wg_, wu_=wu_, wgtok=wgtok, wutok=wutok):
            prog.wait(g, wgtok, wutok)
            for k in range(16):
                prog.dma(g, "qG", wg_[:, 512 * k:512 * k + fw],
                         wgqT_hbm[128 * k:128 * (k + 1), f0:f1])
                prog._l0wl = prog.dma(g, "qG", wu_[:, 512 * k:512 * k + fw],
                                      wuqT_hbm[128 * k:128 * (k + 1), f0:f1])

        for j in range(4):
            sidx = f0 // 128 + j
            gb_tok = emit_col_bcast(prog, cst, wsg_cols[:, sidx:sidx + 1],
                                    wsgb[:, 128 * j:128 * (j + 1)], None)
            ub_tok = emit_col_bcast(prog, cst, wsu_cols[:, sidx:sidx + 1],
                                    wsub[:, 128 * j:128 * (j + 1)], None)

        for ti in range(NT):
            qi_, qx_, qftok = qx.nxt()

            @block.sync
            def _(s, ti=ti, qx_=qx_, qftok=qftok):
                prog.wait(s, qftok)
                prog._qxl = prog.dma(s, "qS", qx_[:], qxT_hbm[ti])

            big, bankg, ftokg = prog.psum_pool.acquire()
            biu, banku, ftoku = prog.psum_pool.acquire()

            @block.tensor
            def _(t, qx_=qx_, wg_=wg_, wu_=wu_, bankg=bankg, banku=banku,
                  ftokg=ftokg, ftoku=ftoku, fw=fw):
                prog.wait(t, prog._qxl, prog._l0wl, ftokg, ftoku)
                for k in range(16):
                    t.matmul(bankg[:, 0:fw], qx_[:, 128 * k:128 * (k + 1)],
                             wg_[:, 512 * k:512 * k + fw],
                             start=(k == 0), stop=(k == 15))
                for k in range(16):
                    mm = t.matmul(banku[:, 0:fw], qx_[:, 128 * k:128 * (k + 1)],
                                  wu_[:, 512 * k:512 * k + fw],
                                  start=(k == 0), stop=(k == 15))
                prog._l0m = prog.inc(mm, "sP")
            qx.done(qi_, prog._l0m)

            gi_, gbuf, gftok = gs.nxt()
            ui_, ubuf, uftok = us.nxt()
            pi_, pb_, pftok = bp.nxt()

            @block.vector
            def _(v, ti=ti, fw=fw, bankg=bankg, banku=banku, gbuf=gbuf,
                  ubuf=ubuf, gftok=gftok, uftok=uftok, biu=biu):
                prog.wait(v, prog._l0m, gftok, uftok, gb_tok, ub_tok)
                v.tensor_scalar(gbuf[:, 0:fw], bankg[:, 0:fw],
                                qsc_cols[:, ti:ti + 1], None, ALU.mult,
                                ALU.bypass)
                v.tensor_tensor(gbuf[:, 0:fw], gbuf[:, 0:fw], wsgb[:, 0:fw],
                                ALU.mult)
                v.tensor_scalar(ubuf[:, 0:fw], banku[:, 0:fw],
                                qsc_cols[:, ti:ti + 1], None, ALU.mult,
                                ALU.bypass)
                prog._l0u = prog.inc(
                    v.tensor_tensor(ubuf[:, 0:fw], ubuf[:, 0:fw], wsub[:, 0:fw],
                                    ALU.mult), "sV")
                prog.psum_pool.release(biu, prog._l0u)

            @block.scalar
            def _(s, fw=fw, gbuf=gbuf, big=big):
                prog.wait(s, prog._l0u)
                tk = prog.inc(s.activation(out=gbuf[:, 0:fw], in_=gbuf[:, 0:fw],
                                           func=AF.Silu), "sS")
                prog.psum_pool.release(big, tk)
                prog._l0sg = tk

            @block.vector
            def _(v, ti=ti, fw=fw, gbuf=gbuf, ubuf=ubuf, pb_=pb_, pftok=pftok,
                  gi_=gi_, ui_=ui_):
                prog.wait(v, prog._l0sg, pftok)
                v.tensor_tensor(pb_[:, 0:fw], gbuf[:, 0:fw], ubuf[:, 0:fw],
                                ALU.mult)
                v.reduce_max(gbuf[:, 0:1], pb_[:, 0:fw], axis=AX.X,
                             apply_absolute_value=True)
                if first_fb[0]:
                    tk = prog.inc(v.tensor_copy(out=rmt_cols[:, ti:ti + 1],
                                                in_=gbuf[:, 0:1]), "sV")
                else:
                    tk = prog.inc(v.tensor_tensor(rmt_cols[:, ti:ti + 1],
                                                  rmt_cols[:, ti:ti + 1],
                                                  gbuf[:, 0:1], ALU.max), "sV")
                gs.done(gi_, tk)
                us.done(ui_, tk)
                prog._l0bp = tk

            @block.sync
            def _(s, ti=ti, f0=f0, f1=f1, pb_=pb_, pi_=pi_, fw=fw):
                prog.wait(s, prog._l0bp)
                bp.done(pi_, prog.dma(s, "qS", base_hbm[ti][:, f0:f1],
                                      pb_[:, 0:fw]))
        first_fb[0] = False
        wg_b.done(wgi, prog._l0m)
        wu_b.done(wui, prog._l0m)


def emit_l0_qt(prog, aps, cst, base_hbm, rmt_cols, qtT_hbm, qscT_cols):
    block = prog.block
    I_ = 5632
    bl = RR([prog.sb("l0bl", (128, I_), F32) for _ in range(2)])
    q8 = prog.sb("l0q8", (128, I_), FP8)
    qb = RR([prog.sb("l0qb", (128, I_), BF16) for _ in range(2)])
    qtT = RR([prog.sb("l0qtT", (128, I_), BF16) for _ in range(2)])
    col = prog.sb("l0qc", (128, 1), F32)
    for ti in range(NT):
        li, b_, lftok = bl.nxt()

        @block.sync
        def _(s, ti=ti, b_=b_, lftok=lftok):
            prog.wait(s, lftok)
            prog._btl = prog.dma(s, "qS", b_[:], base_hbm[ti])

        qi_, qb_, qbftok = qb.nxt()

        @block.vector
        def _(v, ti=ti, b_=b_, qb_=qb_, qbftok=qbftok, li=li):
            prog.wait(v, prog._btl, qbftok)
            v.tensor_scalar_mul(qscT_cols[:, ti:ti + 1], rmt_cols[:, ti:ti + 1],
                                1.0 / 112.0)
            v.reciprocal(col[:, 0:1], rmt_cols[:, ti:ti + 1])
            v.tensor_scalar_mul(col[:, 0:1], col[:, 0:1], 224.0)
            v.tensor_single_scalar(q8[:], b_[:], col[:, 0:1], ALU.mult)
            tk = prog.inc(v.tensor_copy(out=qb_[:], in_=q8[:]), "sV")
            bl.done(li, tk)
            prog._qtq = tk

        tti, qtT_, tftok = qtT.nxt()
        pairs = [(qb_[:, 128 * c:128 * (c + 1)], qtT_[:, 128 * c:128 * (c + 1)],
                  _merge(prog._qtq, tftok)) for c in range(44)]
        ttoks = emit_transposes(prog, cst, pairs)
        qb.done(qi_, ttoks)

        @block.sync
        def _(s, ti=ti, qtT_=qtT_, ttoks=ttoks):
            prog.wait(s, ttoks)
            prog._qts = prog.dma(s, "qS", qtT_hbm[ti], qtT_[:])
        qtT.done(tti, prog._qts)


def emit_l0_down(prog, aps, cst, qtT_hbm, wdqT_hbm, wsd_cols, qscT_cols, g_hbm):
    block = prog.block
    I_ = 5632
    wd = prog.sb("l0wd", (128, 44 * 512), BF16)
    wsdb = prog.sb("wsdb", (128, 512), F32)
    hst = RR([prog.sb("l0hst", (128, I_), BF16) for _ in range(2)])
    gp = RR([(prog.sb("l0gp", (128, 512), F32), prog.sb("l0gpb", (128, 512), F32))
             for _ in range(2)])
    wdtok = [None]
    for fbd in range(4):
        @block.gpsimd
        def _(g, fbd=fbd):
            prog.wait(g, wdtok[0])
            for ic in range(44):
                prog._wdl0 = prog.dma(g, "qG", wd[:, 512 * ic:512 * (ic + 1)],
                                      wdqT_hbm[128 * ic:128 * (ic + 1),
                                               512 * fbd:512 * (fbd + 1)])

        for j in range(4):
            sidx = 4 * fbd + j
            sb_tok = emit_col_bcast(prog, cst, wsd_cols[:, sidx:sidx + 1],
                                    wsdb[:, 128 * j:128 * (j + 1)], None)
        for ti in range(NT):
            hi_, hst_, hftok = hst.nxt()

            @block.sync
            def _(s, ti=ti, hst_=hst_, hftok=hftok):
                prog.wait(s, hftok)
                prog._htl0 = prog.dma(s, "qS", hst_[:], qtT_hbm[ti])

            bi, bank, ftok = prog.psum_pool.acquire()

            @block.tensor
            def _(t, hst_=hst_, bank=bank, ftok=ftok):
                prog.wait(t, prog._htl0, prog._wdl0, ftok)
                for ic in range(44):
                    mm = t.matmul(bank[:, 0:512],
                                  hst_[:, 128 * ic:128 * (ic + 1)],
                                  wd[:, 512 * ic:512 * (ic + 1)],
                                  start=(ic == 0), stop=(ic == 43))
                prog._md0 = prog.inc(mm, "sP")
            hst.done(hi_, prog._md0)

            gi_, (gf, gb), gftok = gp.nxt()

            @block.vector
            def _(v, ti=ti, bank=bank, gf=gf, gftok=gftok):
                prog.wait(v, prog._md0, gftok, sb_tok)
                v.tensor_scalar(gf[:], bank[:, 0:512], qscT_cols[:, ti:ti + 1],
                                None, ALU.mult, ALU.bypass)
                prog._ed0 = prog.inc(
                    v.tensor_tensor(gf[:], gf[:], wsdb[:], ALU.mult), "sV")

            @block.scalar
            def _(s, gf=gf, gb=gb, bi=bi):
                prog.wait(s, prog._ed0)
                tk = prog.inc(s.activation(out=gb[:], in_=gf[:], func=AF.Silu),
                              "sS")
                prog.psum_pool.release(bi, tk)
                prog._gd0 = tk

            @block.sync
            def _(s, ti=ti, fbd=fbd, gb=gb, gi_=gi_):
                prog.wait(s, prog._gd0)
                gp.done(gi_, prog.dma(
                    s, "qS", g_hbm[128 * ti:128 * (ti + 1),
                                   512 * fbd:512 * (fbd + 1)], gb[:]))
        wdtok[0] = prog._md0


def emit_l0_proj_g_dt(prog, aps, cst, xnT, gsilu_hbm, dtT, dtvec):
    block = prog.block
    prog.wbufs_l = RR([prog.sb("l0wbl", (128, 8192), BF16) for _ in range(2)])
    gpieces = RR([(prog.sb("l0gp2", (128, 512), BF16)) for _ in range(3)])

    def evac_g(ti, f0, f1, bank, bi, mtok):
        pi, piece, pftok = gpieces.nxt()

        @block.scalar
        def _(s):
            prog.wait(s, mtok, pftok)
            prog._gse = prog.inc(
                s.activation(out=piece[:, 0:f1 - f0], in_=bank[:, 0:f1 - f0],
                             func=AF.Silu), "sS")
            prog.psum_pool.release(bi, prog._gse)

        @block.sync
        def _(s):
            prog.wait(s, prog._gse)
            gpieces.done(pi, prog.dma(
                s, "qS", gsilu_hbm[ti][:, f0:f1], piece[:, 0:f1 - f0]))

    emit_tm_proj(prog, cst, aps["L0_WprojT"], 3072, 2048, xnT, evac_g)

    prog.wbufs_s = RR([prog.sb("l0wbs", (128, 512), BF16) for _ in range(2)])
    dtb_col = dtvec[0:32, 0:1]
    scr = prog.sb("dtscr", (32, T), F32)

    @block.sync
    def _(s):
        prog._dtl = prog.dma(s, "qS", dtvec[:, 0:4], aps["L0_dtba"])

    wi, wb_, wtok = prog.wbufs_s.nxt()

    @block.gpsimd
    def _(g):
        prog.wait(g, wtok)
        for k in range(16):
            prog._wld0 = prog.dma(g, "qG", wb_[:, 32 * k:32 * (k + 1)],
                                  aps["L0_WprojT"][128 * k:128 * (k + 1),
                                                   5120:5152])

    for tc in range(T // 512):
        bi, bank, ftok = prog.psum_pool.acquire()

        @block.tensor
        def _(t, tc=tc, bank=bank, ftok=ftok):
            prog.wait(t, prog._wld0, ftok, xnT.ready())
            for k in range(16):
                mm = t.matmul(bank[0:32, 0:512], wb_[:, 32 * k:32 * (k + 1)],
                              xnT.blk(k, 512 * tc, 512 * (tc + 1)),
                              start=(k == 0), stop=(k == 15))
            prog._dtm = prog.inc(mm, "sP")

        @block.vector
        def _(v, tc=tc, bank=bank, bi=bi):
            prog.wait(v, prog._dtm, prog._dtl)
            tk = prog.inc(v.tensor_single_scalar(
                dtT[0:32, 512 * tc:512 * (tc + 1)], bank[0:32, 0:512],
                dtb_col[:, 0:1], ALU.add), "sV")
            prog.psum_pool.release(bi, tk)
            prog._dtr = tk

    @block.scalar
    def _(s):
        prog.wait(s, prog._dtr)
        s.activation(out=scr[:], in_=dtT[0:32, :], func=AF.Abs)
        s.activation(out=scr[:], in_=scr[:], func=AF.Exp, scale=-1.0)
        prog._dsp = prog.inc(
            s.activation(out=scr[:], in_=scr[:], func=AF.Ln,
                         bias=cst["ones_col_f"][0:32, 0:1]), "sS")

    @block.vector
    def _(v):
        prog.wait(v, prog._dsp)
        v.tensor_relu(dtT[0:32, :], dtT[0:32, :])
        prog._dtf = prog.inc(
            v.tensor_tensor(dtT[0:32, :], dtT[0:32, :], scr[:], ALU.add), "sV")

    @block.scalar
    def _(s):
        prog.wait(s, prog._dtl)
        prog._aex = prog.inc(s.activation(out=dtvec[0:32, 4:5],
                                          in_=dtvec[0:32, 1:2],
                                          func=AF.Exp), "sS")

    @block.vector
    def _(v):
        prog.wait(v, prog._aex)
        prog._aneg = prog.inc(v.tensor_scalar_mul(dtvec[0:32, 5:6],
                                                  dtvec[0:32, 4:5],
                                                  -1.0), "sV")
    prog.l0_acol = dtvec[0:32, 5:6]
    prog.l0_dtvec = dtvec
    prog.l0_dt_tok = _merge(prog._dtf, prog._aneg)


def emit_l0_proj_vkq(prog, aps, cst, xnT, vkq_hbm):
    block = prog.block
    prog.wbufs_s = RR([prog.sb("l0wbs2", (128, 2048), BF16) for _ in range(2)])
    cw0 = prog.sb("cw0", (128, 24), F32)
    cw1 = prog.sb("cw1", (128, 24), F32)
    cbb = prog.sb("cbb", (128, 24), F32)
    raw = prog.sb("l0raw", (128, T), F32)
    tmp = prog.sb("l0tmp", (128, T), F32)
    strips = RR([prog.sb("l0vst", (128, T), BF16) for _ in range(2)])

    @block.sync
    def _(s):
        prog.dma(s, "qS", cw0[:], aps["L0_cw0"])
        prog.dma(s, "qS", cw1[:], aps["L0_cw1"])
        prog._cwl = prog.dma(s, "qS", cbb[:], aps["L0_cb"])

    for b in range(24):
        wi, wb_, wtok = prog.wbufs_s.nxt()

        @block.gpsimd
        def _(g, b=b, wb_=wb_, wtok=wtok):
            prog.wait(g, wtok)
            for k in range(16):
                prog._vkl = prog.dma(g, "qG", wb_[:, 128 * k:128 * (k + 1)],
                                     aps["L0_WprojT"][128 * k:128 * (k + 1),
                                                      128 * b:128 * (b + 1)])

        toks = []
        for tc in range(T // 512):
            bi, bank, ftok = prog.psum_pool.acquire()

            @block.tensor
            def _(t, tc=tc, wb_=wb_, bank=bank, ftok=ftok):
                prog.wait(t, prog._vkl, ftok, xnT.ready())
                for k in range(16):
                    mm = t.matmul(bank[:, 0:512], wb_[:, 128 * k:128 * (k + 1)],
                                  xnT.blk(k, 512 * tc, 512 * (tc + 1)),
                                  start=(k == 0), stop=(k == 15))
                prog._vkm = prog.inc(mm, "sP")

            @block.vector
            def _(v, tc=tc, bank=bank, bi=bi):
                prog.wait(v, prog._vkm, getattr(prog, "_vkc", None))
                tk = prog.inc(v.tensor_copy(out=raw[:, 512 * tc:512 * (tc + 1)],
                                            in_=bank[:, 0:512]), "sV")
                prog.psum_pool.release(bi, tk)
                toks.append(tk)
        prog.wbufs_s.done(wi, prog._vkm)

        @block.vector
        def _(v, b=b, toks=toks):
            prog.wait(v, toks, prog._cwl, getattr(prog, "_vkc", None))
            v.tensor_scalar(tmp[:], raw[:], cw1[:, b:b + 1], cbb[:, b:b + 1],
                            ALU.mult, ALU.add)
            prog._cv = prog.inc(
                v.scalar_tensor_tensor(tmp[:, 1:T], raw[:, 0:T - 1],
                                       cw0[:, b:b + 1], tmp[:, 1:T],
                                       ALU.mult, ALU.add), "sV")

        si_, st_, sftok = strips.nxt()

        @block.scalar
        def _(s, b=b, st_=st_, sftok=sftok):
            prog.wait(s, prog._cv, sftok)
            prog._vkc = prog.inc(
                s.activation(out=st_[:], in_=tmp[:], func=AF.Silu), "sS")

        @block.sync
        def _(sy, b=b, st_=st_, si_=si_):
            prog.wait(sy, prog._vkc)
            strips.done(si_, prog.dma(
                sy, "qS", vkq_hbm[:, b].rearrange("t p c -> p t c"),
                st_[:].rearrange("p (t c) -> p t c", c=128)))


def emit_l0_ssd(prog, aps, cst, vkq_hbm, dtT, gsilu_hbm, ygT_hbm):
    block = prog.block
    H = 32
    acol = prog.l0_acol
    dttok = prog.l0_dt_tok
    AT = prog.sb("AT0", (32, T), F32)
    lnd = prog.sb("lnd0", (32, T), F32)
    Acs = prog.sb("Acs0", (32, T), F32)
    D1r = prog.sb("D1r0", (32, 128), F32)
    D2r = prog.sb("D2r0", (32, 128), F32)
    rrow = prog.sb("rrow0", (32, 1), F32)
    selc = prog.sb("sel0", (32, 2048), F32)
    gnr = prog.sb("gnr0", (1, 512), F32)
    gnb = prog.sb("gnb0", (128, 64), F32)

    @block.sync
    def _(s):
        prog.dma(s, "qS", selc[:], aps["C_l0sel"])
        prog._gnl = prog.dma(s, "qS", gnr[:], aps["L0_gnw"])
    gn_tok = emit_bcast_row(prog, cst, gnr[0:1, 0:64], 64, gnb, prog._gnl)

    @block.scalar
    def _(s):
        prog.wait(s, dttok)
        prog._lnd = prog.inc(s.activation(out=lnd[:], in_=dtT[0:32, :],
                                          func=AF.Ln), "sS")

    @block.vector
    def _(v):
        prog.wait(v, dttok)
        prog._at = prog.inc(v.tensor_single_scalar(AT[:], dtT[0:32, :],
                                                   acol, ALU.mult), "sV")

    for c in range(NT):
        @block.vector
        def _(v, c=c):
            prog.wait(v, prog._at)
            prog._acs = prog.inc(
                v.tensor_tensor_scan(Acs[:, 128 * c:128 * (c + 1)],
                                     AT[:, 128 * c:128 * (c + 1)],
                                     AT[:, 128 * c:128 * (c + 1)], 0.0,
                                     ALU.add, ALU.bypass), "sV")

    Rt = [prog.sb(f"R0_{j}", (128, 64), F32) for j in range(16)]

    @block.vector
    def _(v):
        last = None
        for j in range(16):
            last = v.memset(Rt[j][:], 0.0)
        prog._r0 = prog.inc(last, "sV")

    Rupd = [None] * 32
    vkb = RR([prog.sb("vkb0", (64, 48 * 128), BF16) for _ in range(2)])
    D1c = prog.sb("D1c0", (128, 32), F32)
    D2c = prog.sb("D2c0", (128, 32), F32)
    rex = prog.sb("rex0", (128, 16), F32)
    Xt = prog.sb("Xt0", (128, 512), BF16)
    Bt = prog.sb("Bt0", (128, 512), BF16)
    XD = prog.sb("XD0", (128, 64), BF16)
    P2 = prog.sb("P20", (128, 128), BF16)
    Rb = prog.sb("Rb0", (64, 64), BF16)
    Ych = prog.sb("Ych0", (128, 2048), F32)
    Ysq = prog.sb("Ysq0", (128, 2048), F32)
    Ygb = prog.sb("Ygb0", (128, 2048), BF16)
    gsl = RR([prog.sb("gsl0", (128, 2048), BF16) for _ in range(2)])
    ygT = RR([prog.sb("ygT0", (128, 2048), BF16) for _ in range(2)])

    if DBG:
        dbg_rows = prog.dram("dbg_rows", (4, 32, T), F32, kind="ExternalOutput")
        dbg_vec = prog.dram("dbg_vec", (128, 8), F32, kind="ExternalOutput")

        @block.sync
        def _(sy):
            prog.wait(sy, dttok)
            prog.dma(sy, "qS", dbg_vec, prog.l0_dtvec[:])
        dbg_cols = prog.dram("dbg_cols", (NT, 128, 80), F32, kind="ExternalOutput")

        @block.sync
        def _(sy):
            prog.wait(sy, prog._acs, prog._lnd, dttok)
            prog.dma(sy, "qS", dbg_rows[0], dtT[0:32, :])
            prog.dma(sy, "qS", dbg_rows[1], lnd[:])
            prog.dma(sy, "qS", dbg_rows[2], Acs[:])
            prog.dma(sy, "qS", dbg_rows[3], AT[:])
    for c in range(NT):
        cs = slice(128 * c, 128 * (c + 1))
        vki, vk_, vkftok = vkb.nxt()

        @block.sync
        def _(sy, c=c, vk_=vk_, vkftok=vkftok):
            prog.wait(sy, vkftok)
            prog._vkl2 = prog.dma(
                sy, "qS", vk_[:].rearrange("p (s c) -> p s c", c=128),
                vkq_hbm[c].rearrange("b (h p) c -> p (b h) c", h=2))

        def VK(vk_, blk, off):
            i2 = 2 * blk + off // 64
            return vk_[0:64, 128 * i2:128 * (i2 + 1)]
        @block.vector
        def _(v, c=c):
            prog.wait(v, prog._acs, prog._lnd, getattr(prog, "_d1used", None))
            prog._d1a = prog.inc(
                v.tensor_tensor(D1r[:], lnd[:, cs], Acs[:, cs], ALU.subtract),
                "sV")

        @block.scalar
        def _(s, c=c):
            prog.wait(s, prog._d1a)
            s.activation(out=D1r[:], in_=D1r[:], func=AF.Exp)
            prog._d2a = prog.inc(
                s.activation(out=D2r[:], in_=Acs[:, cs], func=AF.Exp), "sS")

        bi, bank, ftok = prog.psum_pool.acquire()

        @block.tensor
        def _(t, bank=bank, ftok=ftok):
            prog.wait(t, prog._d2a, ftok, getattr(prog, "_rexe", None))
            t.matmul(bank[0:128, 0:32], D1r[:], cst["idf"][0:32, 0:32],
                     start=True, stop=True)
            t.matmul(bank[0:128, 32:64], D2r[:], cst["idf"][0:32, 0:32],
                     start=True, stop=True)
            for j in range(16):
                mm = t.matmul(bank[0:128, 64 + j:65 + j],
                              selc[:, 128 * j:128 * (j + 1)],
                              D2r[:, 127:128], start=True, stop=True)
            prog._dtr2 = prog.inc(mm, "sP")

        @block.vector
        def _(v, bank=bank, bi=bi):
            prog.wait(v, prog._dtr2)
            v.tensor_copy(out=D1c[:], in_=bank[0:128, 0:32])
            v.tensor_copy(out=D2c[:], in_=bank[0:128, 32:64])
            tk = prog.inc(v.tensor_copy(out=rex[:], in_=bank[0:128, 64:80]),
                          "sV")
            prog.psum_pool.release(bi, tk)
            prog._rexe = tk

        if DBG:
            @block.sync
            def _(sy, c=c):
                prog.wait(sy, prog._rexe)
                prog.dma(sy, "qS", dbg_cols[c][:, 0:32], D1c[:])
                prog.dma(sy, "qS", dbg_cols[c][:, 32:64], D2c[:])
                prog.dma(sy, "qS", dbg_cols[c][:, 64:80], rex[:])

        xb_in_tok = _merge(prog._vkl2, getattr(prog, "_xbused", None))
        xbtoks = []
        for grp in range(2):
            bi_x, bank_x, ftok_x = prog.psum_pool.acquire()

            @block.tensor
            def _(t, grp=grp, vk_=vk_, bank_x=bank_x, ftok_x=ftok_x):
                prog.wait(t, xb_in_tok, ftok_x)
                for j in range(8):
                    kvh = 8 * grp + j if False else j
                    src = VK(vk_, (4 * grp) + j // 2, 64 * (j % 2))
                    mm = t.matmul(bank_x[:, 64 * j:64 * (j + 1)], src,
                                  cst["idb"][0:64, 0:64], start=True, stop=True)
                prog._xbt = prog.inc(mm, "sP")

            @block.vector
            def _(v, grp=grp, bank_x=bank_x, bi_x=bi_x):
                prog.wait(v, prog._xbt)
                dst = Xt if grp == 0 else Bt
                tk = prog.inc(v.tensor_copy(out=dst[:, 0:512],
                                            in_=bank_x[:, 0:512]), "sV")
                prog.psum_pool.release(bi_x, tk)
                xbtoks.append(tk)

        gi_, gs_, gftok = gsl.nxt()

        @block.sync
        def _(s, c=c, gs_=gs_, gftok=gftok):
            prog.wait(s, gftok)
            prog._gsl = prog.dma(s, "qS", gs_[:], gsilu_hbm[c])

        ytoks = []
        sdtoks = []
        for h in range(H):
            kv = h // 4
            qblk = 8 + h // 2
            qoff = 64 * (h % 2)
            kblk = 4 + kv // 2
            koff = 64 * (kv % 2)

            biy, banky, ftoky = prog.psum_pool.acquire()
            bis, banks, ftoks_ = prog.psum_pool.acquire()

            @block.tensor
            def _(t, c=c, h=h, banky=banky, banks=banks, ftoky=ftoky,
                  ftoks_=ftoks_, qblk=qblk, qoff=qoff, kblk=kblk, koff=koff,
                  vk_=vk_):
                prog.wait(t, prog._vkl2, xbtoks, ftoky, ftoks_)
                prog._sst = prog.inc(
                    t.matmul(banks[:, 0:128], VK(vk_, kblk, koff),
                             VK(vk_, qblk, qoff),
                             start=True, stop=True), "sP")

            @block.vector
            def _(v, h=h, banks=banks, bis=bis):
                prog.wait(v, prog._sst, prog._rexe, getattr(prog, "_p2u", None))
                v.tensor_single_scalar(P2[:], banks[:, 0:128], D1c[:, h:h + 1],
                                       ALU.mult)
                tk = prog.inc(v.tensor_tensor(P2[:], P2[:], cst["triu128"][:],
                                              ALU.mult), "sV")
                prog.psum_pool.release(bis, tk)
                prog._p2 = tk

            @block.vector
            def _(v, h=h, kv=kv):
                prog.wait(v, xbtoks, prog._rexe, getattr(prog, "_xdu", None))
                prog._xd = prog.inc(
                    v.tensor_single_scalar(XD[:], Xt[:, 64 * kv:64 * kv + 64],
                                           D1c[:, h:h + 1], ALU.mult), "sV")

            bisd, banksd, ftoksd = prog.psum_pool.acquire()

            @block.vector
            def _(v, h=h):
                ro = 64 * (h % 2)
                prog.wait(v, prog._r0, Rupd[h], getattr(prog, "_rbu", None))
                prog._rb = prog.inc(
                    v.tensor_copy(out=Rb[:, :],
                                  in_=Rt[h // 2][ro:ro + 64, :]), "sV")

            @block.tensor
            def _(t, c=c, h=h, kv=kv, banky=banky, banksd=banksd,
                  ftoksd=ftoksd, qblk=qblk, qoff=qoff, vk_=vk_):
                prog.wait(t, prog._p2, prog._xd, ftoksd, prog._rb)
                t.matmul(banky[:, 0:64], P2[:], Xt[:, 64 * kv:64 * kv + 64],
                         start=True, stop=False)
                prog._yo = prog.inc(
                    t.matmul(banky[:, 0:64], VK(vk_, qblk, qoff), Rb[:, :],
                             start=False, stop=True), "sP")
                prog._sd = prog.inc(
                    t.matmul(banksd[0:64, 0:64], Bt[:, 64 * kv:64 * kv + 64],
                             XD[:], start=True, stop=True), "sP")
            prog._p2u = prog._yo
            prog._xdu = prog._sd
            prog._rbu = prog._yo

            @block.vector
            def _(v, h=h, banky=banky, biy=biy):
                prog.wait(v, prog._yo)
                tk = prog.inc(
                    v.tensor_single_scalar(Ych[:, 64 * h:64 * (h + 1)],
                                           banky[:, 0:64], D2c[:, h:h + 1],
                                           ALU.mult), "sV")
                prog.psum_pool.release(biy, tk)
                ytoks.append(tk)

            @block.vector
            def _(v, h=h, banksd=banksd, bisd=bisd):
                prog.wait(v, prog._sd)
                ro = 64 * (h % 2)
                R = Rt[h // 2]
                v.tensor_tensor(R[ro:ro + 64, :], R[ro:ro + 64, :],
                                banksd[0:64, 0:64], ALU.add)
                tk = prog.inc(
                    v.tensor_single_scalar(R[ro:ro + 64, :], R[ro:ro + 64, :],
                                           rex[ro:ro + 64, h // 2:h // 2 + 1],
                                           ALU.mult), "sV")
                prog.psum_pool.release(bisd, tk)
                sdtoks.append(tk)
                Rupd[h] = tk
        prog._xbused = _merge(ytoks, sdtoks)
        vkb.done(vki, prog._xbused)

        @block.vector
        def _(v, c=c, gs_=gs_):
            prog.wait(v, ytoks, prog._gsl, gn_tok, getattr(prog, "_ygu", None))
            v.tensor_tensor(Ysq[:], Ych[:], Ych[:], ALU.mult)
            v.reduce_sum(D1c[:, 0:32], Ysq[:].rearrange("p (h e) -> p h e", e=64),
                         axis=AX.X)
            prog._gn1 = prog.inc(v.tensor_copy(out=D1c[:, 0:32],
                                               in_=D1c[:, 0:32]), "sV")

        @block.scalar
        def _(s):
            prog.wait(s, prog._gn1)
            s.activation(out=D2c[:, 0:32], in_=D1c[:, 0:32], func=AF.Ln,
                         scale=1.0 / 64.0, bias=cst["c_eps1e5"][:, 0:1])
            prog._gn2 = prog.inc(s.activation(out=D2c[:, 0:32], in_=D2c[:, 0:32],
                                              func=AF.Exp, scale=-0.5), "sS")

        @block.vector
        def _(v, gs_=gs_, gi_=gi_):
            prog.wait(v, prog._gn2)
            v.tensor_tensor(Ych[:].rearrange("p (h e) -> p h e", e=64),
                            Ych[:].rearrange("p (h e) -> p h e", e=64),
                            D2c[:, 0:32].unsqueeze(2).to_broadcast([128, 32, 64]),
                            ALU.mult)
            v.tensor_tensor(Ych[:].rearrange("p (h e) -> p h e", e=64),
                            Ych[:].rearrange("p (h e) -> p h e", e=64),
                            gnb[:].unsqueeze(1).to_broadcast([128, 32, 64]),
                            ALU.mult)
            tk = prog.inc(v.tensor_tensor(Ygb[:], Ych[:], gs_[:], ALU.mult),
                          "sV")
            gsl.done(gi_, tk)
            prog._yg = tk
            prog._d1used = _merge(ytoks, sdtoks, tk, prog._gn2)

        yi_, ygT_, ygftok = ygT.nxt()
        pairs = [(Ygb[:, 128 * j:128 * (j + 1)], ygT_[:, 128 * j:128 * (j + 1)],
                  _merge(prog._yg, ygftok)) for j in range(16)]
        yttoks = emit_transposes(prog, cst, pairs)
        prog._ygu = yttoks

        @block.sync
        def _(s, c=c, ygT_=ygT_, yttoks=yttoks):
            prog.wait(s, yttoks)
            prog._ygs = prog.dma(s, "qS", ygT_hbm[c], ygT_[:])
        ygT.done(yi_, prog._ygs)


def emit_l0_outproj(prog, aps, cst, ygT_hbm, s_hbm):
    block = prog.block
    wb = RR([prog.sb("l0ow", (128, 8192), BF16) for _ in range(2)])
    yg = RR([prog.sb("l0oy", (128, 2048), BF16) for _ in range(2)])
    sp = RR([prog.sb("l0osp", (128, 512), F32) for _ in range(3)])
    for f0 in range(0, D, 512):
        wi, wb_, wtok = wb.nxt()

        @block.gpsimd
        def _(g, f0=f0, wb_=wb_, wtok=wtok):
            prog.wait(g, wtok)
            for k in range(16):
                prog._owl = prog.dma(g, "qG", wb_[:, 512 * k:512 * (k + 1)],
                                     aps["L0_WoT"][128 * k:128 * (k + 1),
                                                   f0:f0 + 512])

        for ti in range(NT):
            yi_, yg_, yftok = yg.nxt()

            @block.sync
            def _(s, ti=ti, yg_=yg_, yftok=yftok):
                prog.wait(s, yftok)
                prog._oyl = prog.dma(s, "qS", yg_[:], ygT_hbm[ti])

            bi, bank, ftok = prog.psum_pool.acquire()

            @block.tensor
            def _(t, yg_=yg_, wb_=wb_, bank=bank, ftok=ftok):
                prog.wait(t, prog._oyl, prog._owl, ftok)
                for k in range(16):
                    mm = t.matmul(bank[:, 0:512], yg_[:, 128 * k:128 * (k + 1)],
                                  wb_[:, 512 * k:512 * (k + 1)],
                                  start=(k == 0), stop=(k == 15))
                prog._om0 = prog.inc(mm, "sP")
            yg.done(yi_, prog._om0)

            pi, piece, pftok = sp.nxt()

            @block.vector
            def _(v, bank=bank, bi=bi, piece=piece, pftok=pftok):
                prog.wait(v, prog._om0, pftok)
                tk = prog.inc(v.tensor_copy(out=piece[:], in_=bank[:, 0:512]),
                              "sV")
                prog.psum_pool.release(bi, tk)
                prog._ose = tk

            @block.sync
            def _(s, ti=ti, f0=f0, piece=piece, pi=pi):
                prog.wait(s, prog._ose)
                sp.done(pi, prog.dma(
                    s, "qS", s_hbm[128 * ti:128 * (ti + 1), f0:f0 + 512],
                    piece[:]))
        wb.done(wi, prog._om0)


def stage_layer0(inputs):
    a = {}
    a["L0_WprojT"] = _f32(inputs["L0_attn_W_vkqgdt"]).T.copy()
    a["L0_WoT"] = _f32(inputs["L0_attn_W_o"]).T.copy()
    cw = _f32(inputs["L0_attn_conv_w"])
    a["L0_cw0"] = cw[:, 0].reshape(24, 128).T.copy()
    a["L0_cw1"] = cw[:, 1].reshape(24, 128).T.copy()
    a["L0_cb"] = _f32(inputs["L0_attn_conv_b"]).reshape(24, 128).T.copy()
    dtba = np.zeros((128, 4), dtype=np.float32)
    dtba[0:32, 0] = _f32(inputs["L0_attn_dt_bias"])
    dtba[0:32, 1] = _f32(inputs["L0_attn_A_log_bias"])
    a["L0_dtba"] = dtba
    gnw = np.zeros((1, 512), dtype=np.float32)
    gnw[0, 0:64] = _f32(inputs["L0_attn_gnorm_w"])
    a["L0_gnw"] = gnw
    a["L0_wg"] = _f32(inputs["L0_ffn_w_gate"])
    a["L0_wu"] = _f32(inputs["L0_ffn_w_up"])
    a["L0_wd"] = _f32(inputs["L0_ffn_w_down"])
    a["L0_didx"] = np.ascontiguousarray(
        np.asarray(inputs["L0_ffn_down_reorder_index"]).astype(np.int32)
        .reshape(5632, 1))
    sel = np.zeros((32, 16 * 128), dtype=np.float32)
    for j in range(16):
        sel[2 * j, 128 * j:128 * j + 64] = 1.0
        sel[2 * j + 1, 128 * j + 64:128 * (j + 1)] = 1.0
    a["C_l0sel"] = sel
    return a


LAYER_STAGERS[0] = stage_layer0
LAYER_EMITTERS[0] = emit_layer0
